# version 03
# baseline (speedup 1.0000x reference)
import math
import os
from contextlib import ExitStack

os.environ.setdefault("GAUGE_NTFF_IGNORE_DMA", "1")

import numpy as np
import ml_dtypes
import concourse.bass as bass
import concourse.mybir as mybir
from concourse.bass_utils import run_bass_kernel_spmd

F32 = mybir.dt.float32
BF16 = mybir.dt.bfloat16
AF = mybir.ActivationFunctionType
ALU = mybir.AluOpType
AX = mybir.AxisListType.X

CFG = dict(
    T=2048,
    NB=2,
    NCORES=8,
    PHASES=("ffn1", "attn", "ffn2"),
    DEBUG=(),
    NE=16,
)

D = 1024
NDT = 8
F = 4096
E = 16
DKH = 64
PM = 128
ACH = 128
EPS_ATT = 1e-6
EPS_LN = 1e-5
LN_SQRT_M = 0.5 * math.log(PM)
INV_DK4 = 1.0 / (DKH ** 0.25)


def build_program(cfg=None):
    cfg = dict(CFG if cfg is None else cfg)
    T = cfg["T"]
    NB = cfg["NB"]
    NTC = T // 512
    NCH = T // ACH
    PHASES = cfg["PHASES"]
    DEBUG = set(cfg["DEBUG"])

    nc = bass.Bass("TRN2", target_bir_lowering=False)

    xin = nc.dram_tensor("L0_attn_x", [NB, T, D], F32, kind="ExternalInput").ap()
    wq_d = nc.dram_tensor("L0_attn_Wq", [D, D], F32, kind="ExternalInput").ap()
    bq_d = nc.dram_tensor("L0_attn_bq", [D], F32, kind="ExternalInput").ap()
    wk_d = nc.dram_tensor("L0_attn_Wk", [D, D], F32, kind="ExternalInput").ap()
    bk_d = nc.dram_tensor("L0_attn_bk", [D], F32, kind="ExternalInput").ap()
    wv_d = nc.dram_tensor("L0_attn_Wv", [D, D], F32, kind="ExternalInput").ap()
    bv_d = nc.dram_tensor("L0_attn_bv", [D], F32, kind="ExternalInput").ap()
    wo_d = nc.dram_tensor("L0_attn_Wo", [D, D], F32, kind="ExternalInput").ap()
    bo_d = nc.dram_tensor("L0_attn_bo", [D], F32, kind="ExternalInput").ap()
    proj_d = nc.dram_tensor("L0_attn_proj", [PM, DKH], F32, kind="ExternalInput").ap()
    keys_d = nc.dram_tensor("L0_ffn_keys", [E, D], F32, kind="ExternalInput").ap()
    w1_d = nc.dram_tensor("L0_ffn_w1", [E, F, D], F32, kind="ExternalInput").ap()
    b1_d = nc.dram_tensor("L0_ffn_b1", [E, F], F32, kind="ExternalInput").ap()
    w2_d = nc.dram_tensor("L0_ffn_w2", [E, D, F], F32, kind="ExternalInput").ap()
    b2_d = nc.dram_tensor("L0_ffn_b2", [E, D], F32, kind="ExternalInput").ap()
    rw_d = nc.dram_tensor("L0_ffn_rw", [E, D], F32, kind="ExternalInput").ap()
    rb_d = nc.dram_tensor("L0_ffn_rb", [E], F32, kind="ExternalInput").ap()
    yout = nc.dram_tensor("y", [NB, T, D], F32, kind="ExternalOutput").ap()

    w1b = nc.dram_tensor("w1b", [E, F, D], BF16, kind="Internal").ap()
    w2b = nc.dram_tensor("w2b", [E, D, F], BF16, kind="Internal").ap()
    wqb = nc.dram_tensor("wqb", [D, D], BF16, kind="Internal").ap()
    wkb = nc.dram_tensor("wkb", [D, D], BF16, kind="Internal").ap()
    wvb = nc.dram_tensor("wvb", [D, D], BF16, kind="Internal").ap()
    wob = nc.dram_tensor("wob", [D, D], BF16, kind="Internal").ap()
    wgtd = nc.dram_tensor("wgtd", [2 * NB, E, T], F32, kind="Internal").ap()

    ctx = ExitStack()

    def sbt(name, shape, dt=F32):
        return ctx.enter_context(nc.sbuf_tensor(name, shape, dt))

    def psb(name, shape, dt=F32):
        return ctx.enter_context(nc.psum_tensor(name, shape, dt))

    dbg_outs = {}

    with ctx:
        consts = sbt("consts", [128, 8])
        xT = sbt("xT", [128, NDT * T])
        ylnT = sbt("ylnT", [128, NDT * T], BF16)
        poolA = sbt("poolA", [128, 16384], BF16)
        poolB = sbt("poolB", [128, 4096], BF16)
        poolC = sbt("poolC", [128, 8192], BF16)
        attn_dqT = sbt("attn_dqT", [128, T], BF16)
        wotb = sbt("wotb", [128, 1024], BF16)
        stage = sbt("stage", [128, 1024])
        xsq = sbt("xsq", [128, 1024])
        wgtbc = sbt("wgtbc", [128, 2048])
        iden_f = sbt("iden_f", [128, 128])
        iden_b = sbt("iden_b", [128, 128], BF16)
        triu_b = sbt("triu_b", [128, 128], BF16)
        ones_r = sbt("ones_r", [1, 128])
        negones_r = sbt("negones_r", [1, 128])
        ones_1 = sbt("ones_1", [1, 8])
        meanw = sbt("meanw", [128, 1])
        halfsq_b = sbt("halfsq_b", [128, 1], BF16)
        projT = sbt("projT", [128, 128], BF16)
        bqcol = sbt("bqcol", [128, 8])
        bkcol = sbt("bkcol", [128, 8])
        bocol = sbt("bocol", [128, 8])
        bvrow = sbt("bvrow", [1, 1024])
        b1colT = sbt("b1colT", [128, 512])
        b2nat = sbt("b2nat", [16, 1024])
        rbcol = sbt("rbcol", [16, 1])
        kssum = sbt("kssum", [16, 1])
        brtcol = sbt("brtcol", [16, 1])
        brtrow = sbt("brtrow", [1, 16])
        wroute_b = sbt("wroute_b", [16, 1024], BF16)
        wrouteT = sbt("wrouteT", [128, 128], BF16)
        Lsb = sbt("Lsb", [128, 16])
        Lwk = sbt("Lwk", [128, 16])
        Mtop = sbt("Mtop", [128, 4])
        masks = sbt("masks", [128, 64])
        Etop = sbt("Etop", [128, 4])
        dsum = sbt("dsum", [128, 1])
        winv = sbt("winv", [128, 1])
        Ptop = sbt("Ptop", [128, 4])
        negm1 = sbt("negm1", [128, 1])
        wgt_tok = sbt("wgt_tok", [128, 16])
        wgt16 = sbt("wgt16", [16, 128])
        wgtrow = sbt("wgtrow", [1, 1024])
        wgt16b = sbt("wgt16b", [16, 512])
        sqrow = sbt("sqrow", [1, 1024])
        keysb = wgtbc[0:16, 0:1024]
        rwb = wgtbc[0:16, 1024:2048]
        wroute = xsq[0:16, 0:1024]
        mrow = wgtrow[0:1, 0:512]
        srow = wgtrow[0:1, 512:1024]
        arow = sqrow[0:1, 0:512]
        brow = sqrow[0:1, 512:1024]
        denrow = sbt("denrow", [1, 128])
        rcprow = sbt("rcprow", [1, 128])
        rbsb = sbt("rbsb", [128, 128])
        Gsb = sbt("Gsb", [128, 65], BF16)

        PS = [psb(f"ps{i}", [128, 512]) for i in range(7)]
        PT = psb("psT", [128, 1024], BF16)

        CONSTS = {-1.0: 0, 0.5: 1, -1.0e30: 2, float(EPS_ATT): 3, float(EPS_LN): 4,
                  -float(LN_SQRT_M): 5, -0.5: 6, float(INV_DK4): 7}
        for val, i in CONSTS.items():
            nc.const_aps.register(F32, float(val), consts[:, i:i + 1])

        ops = {e: [] for e in "tsvgy"}
        sems = {}

        def sget(name):
            if name not in sems:
                sems[name] = [ctx.enter_context(nc.semaphore(name)), 0]
            return sems[name]

        def snow(name):
            return sget(name)[1]

        def em(eng, fn, inc=None, inc_by=1, waits=()):
            tgt = None
            inch = None
            if inc is not None:
                rec = sget(inc)
                rec[1] += inc_by
                tgt = rec[1]
                inch = rec[0]
            wl = [(sget(s)[0], v) for (s, v) in waits if v and v > 0]

            def run(h, wl=wl, fn=fn, inch=inch, inc_by=inc_by):
                for sh_, v in wl:
                    h.wait_ge(sh_, v)
                ins = fn(h)
                if inch is not None:
                    ins.then_inc(inch, inc_by)
            ops[eng].append(run)
            return tgt

        pbr = [0] * 8

        def bw(i):
            return (f"pb{i}", pbr[i])

        def br(i):
            pbr[i] += 1
            return f"pb{i}"

        def bn(i):
            return snow(f"pb{i}")

        def MM(lhsT, rhs, out_bank, out_ap, start, stop, inc=None, waits=()):
            w = list(waits)
            if start:
                w.append(bw(out_bank))
            return em("t", lambda t, l=lhsT, r=rhs, o=out_ap, st_=start, sp=stop:
                      t.matmul(o, l, r, start=st_, stop=sp, skip_group_check=True),
                      inc=inc, waits=w)

        def dump(name, ap, shape, dt=F32, waits=()):
            if name not in DEBUG:
                return
            d = nc.dram_tensor("dbg_" + name, list(shape), dt,
                               kind="ExternalOutput").ap()
            dbg_outs[name] = "dbg_" + name
            em("g", lambda g, d=d, ap=ap: g.dma_start(d, ap), inc="dbg",
               inc_by=16, waits=waits)

        ffn_st = dict(w1_idx=0, slabs=[], slots=[], wrow={0: 0, 1: 0})

        def setup():
            for val, i in CONSTS.items():
                em("v", lambda v, val=val, i=i: v.memset(consts[:, i:i + 1], float(val)))
            em("v", lambda v: v.memset(ones_r[:], 1.0))
            em("v", lambda v: v.memset(negones_r[:], -1.0))
            em("v", lambda v: v.memset(ones_1[:], 1.0))
            em("v", lambda v: v.memset(meanw[:], 1.0 / D))
            t_ms = em("v", lambda v: v.memset(halfsq_b[:], 0.0625), inc="pv")

            iorow = xsq[:, 0:128]
            iocol = xsq[:, 128:256]
            t_io = em("g", lambda g: g.iota(iorow, [[0, 128]], channel_multiplier=1,
                                            allow_small_or_imprecise_dtypes=True), inc="io")
            t_io = em("g", lambda g: g.iota(iocol, [[1, 128]], channel_multiplier=0,
                                            allow_small_or_imprecise_dtypes=True), inc="io")
            em("v", lambda v: v.tensor_tensor(iden_f[:], iorow, iocol, ALU.is_equal),
               inc="pv", waits=[("io", t_io)])
            em("v", lambda v: v.tensor_tensor(iden_b[:], iorow, iocol, ALU.is_equal),
               inc="pv")
            t_id = em("v", lambda v: v.tensor_tensor(triu_b[:], iorow, iocol, ALU.is_le),
                      inc="pv")

            for e in range(E):
                em("g", lambda g, e=e: g.dma_start(w1b[e], w1_d[e]),
                   inc=f"cast{e}", inc_by=16)
                em("g", lambda g, e=e: g.dma_start(w2b[e], w2_d[e]),
                   inc=f"cast{e}", inc_by=16)
            for dst, src in ((wqb, wq_d), (wkb, wk_d), (wvb, wv_d), (wob, wo_d)):
                em("g", lambda g, d=dst, s=src: g.dma_start(d, s),
                   inc="castA", inc_by=16)

            em("g", lambda g: g.dma_start(keysb[:], keys_d), inc="ld", inc_by=16)
            em("g", lambda g: g.dma_start(rwb[:], rw_d), inc="ld", inc_by=16)
            em("g", lambda g: g.dma_start(
                rbcol[:], rb_d.rearrange("(e a) -> e a", a=1)), inc="ld", inc_by=16)
            em("g", lambda g: g.dma_start(b2nat[:], b2_d), inc="ld", inc_by=16)
            em("g", lambda g: g.dma_start(stage[:, 0:64], proj_d), inc="ld", inc_by=16)
            t_ld = em("g", lambda g: g.dma_start(
                bvrow[:], bv_d.rearrange("(a d) -> a d", a=1)), inc="ld", inc_by=16)
            t_bv = t_ld

            t1 = em("v", lambda v: v.tensor_scalar_mul(stage[:, 64:128], stage[:, 0:64],
                                                       float(INV_DK4)),
                    inc="pv", waits=[("ld", t_ld)])
            t1 = em("v", lambda v: v.tensor_copy(poolB[:, 0:64], stage[:, 64:128]),
                    inc="pv", waits=[("pv", t1)])
            t2 = em("t", lambda t: t.transpose(PT[0:64, 0:128], poolB[:, 0:64],
                                               iden_b[:]),
                    inc="pt", waits=[("pv", t1), bw(7)])
            em("v", lambda v: v.tensor_copy(projT[0:64, :], PT[0:64, 0:128]),
               inc=br(7), waits=[("pt", t2)])
            em("v", lambda v: v.tensor_copy(projT[64:128, :], PT[0:64, 0:128]),
               inc=br(7), waits=[("pt", t2)])
            t_projT = bn(7)

            t4 = em("v", lambda v: v.scalar_tensor_tensor(wroute[:], keysb[:], 2.0,
                                                          rwb[:], ALU.mult, ALU.add),
                    inc="pv", waits=[("ld", t_ld), ("pv", t_id)])
            t4b = em("v", lambda v: v.tensor_copy(wroute_b[:], wroute[:]),
                     inc="pv", waits=[("pv", t4)])
            t5 = em("s", lambda s: s.activation(stage[0:16, 0:1024], keysb[:],
                                                AF.Square),
                    inc="psc", waits=[("ld", t_ld), ("pv", t1)])
            t5 = em("v", lambda v: v.reduce_sum(kssum[:], stage[0:16, 0:1024], axis=AX),
                    inc="pv", waits=[("psc", t5)])
            t5 = em("v", lambda v: v.tensor_sub(brtcol[:], rbcol[:], kssum[:]),
                    inc="pv", waits=[("pv", t5)])
            t6 = em("t", lambda t: t.matmul(PS[0][0:1, 0:16], brtcol[:],
                                            iden_f[0:16, 0:16], start=True, stop=True,
                                            skip_group_check=True),
                    inc="pt", waits=[("pv", t5), ("pv", t_id), bw(0)])
            em("v", lambda v: v.tensor_copy(brtrow[:], PS[0][0:1, 0:16]),
               inc=br(0), waits=[("pt", t6)])
            t_broute = bn(0)

            t_wrT = 0
            for dt in range(NDT):
                t7 = em("t", lambda t, dt=dt: t.transpose(
                    PT[0:128, 0:16], wroute_b[:, dt * 128:(dt + 1) * 128],
                    iden_b[0:16, 0:16]),
                    inc="pt", waits=[("pv", t4b), bw(7)])
                em("v", lambda v, dt=dt: v.tensor_copy(
                    wrouteT[:, dt * 16:(dt + 1) * 16], PT[0:128, 0:16]),
                   inc=br(7), waits=[("pt", t7)])
            t_wrT = bn(7)

            for bsrc, bcol in ((bq_d, bqcol), (bk_d, bkcol), (bo_d, bocol)):
                t8 = em("g", lambda g, b=bsrc: g.dma_start(
                    stage[0:1, 0:1024], b.rearrange("(a d) -> a d", a=1)),
                    inc="ld", inc_by=16,
                    waits=[("pv", snow("pv")), ("pt", snow("pt"))])
                for dt in range(NDT):
                    t9 = em("t", lambda t, dt=dt: t.matmul(
                        PS[1][0:128, 0:1], stage[0:1, dt * 128:(dt + 1) * 128],
                        ones_1[0:1, 0:1], start=True, stop=True,
                        skip_group_check=True),
                        inc="pt", waits=[("ld", t8), bw(1)])
                    em("v", lambda v, dt=dt, c=bcol: v.tensor_copy(
                        c[:, dt:dt + 1], PS[1][0:128, 0:1]),
                       inc=br(1), waits=[("pt", t9)])
            for piece in range(4):
                t10 = em("g", lambda g, p=piece: g.dma_start(
                    stage[0:16, 0:1024], b1_d[:, p * 1024:(p + 1) * 1024]),
                    inc="b1ld", inc_by=16,
                    waits=[("pt", snow("pt")), ("pv", snow("pv"))])
                for j in range(8):
                    ft = piece * 8 + j
                    t11 = em("t", lambda t, j=j: t.matmul(
                        PS[1][0:128, 0:16], stage[0:16, j * 128:(j + 1) * 128],
                        iden_f[0:16, 0:16], start=True, stop=True,
                        skip_group_check=True),
                        inc="pt", waits=[("b1ld", t10), bw(1)])
                    em("v", lambda v, ft=ft: v.tensor_copy(
                        b1colT[:, ft * 16:(ft + 1) * 16], PS[1][0:128, 0:16]),
                       inc=br(1), waits=[("pt", t11)])
            t_bias = bn(1)
            return dict(projT=t_projT, broute=t_broute, wrouteT=t_wrT, bias=t_bias,
                        bvrow=t_bv, ld=snow("ld"),
                        stage_users=[("pt", snow("pt")), ("psc", snow("psc"))],
                        cast_qkv=snow("cast"))

        def load_x(b, stage_free, xT_free):
            tch_last = 0
            for tch in range(T // 128):
                w = list(stage_free) + [("xtp", tch_last), ("std", snow("std"))]
                for hlf in range(2):
                    tin = em("g", lambda g, b=b, tch=tch, h=hlf: g.dma_start(
                        stage[:, h * 512:(h + 1) * 512],
                        xin[b, tch * 128:(tch + 1) * 128, h * 512:(h + 1) * 512]),
                        inc="xin", inc_by=16, waits=(w if hlf == 0 else []))
                for dt in range(NDT):
                    ttp = em("t", lambda t, dt=dt: t.transpose(
                        PS[6][0:128, 0:128],
                        stage[:, dt * 128:(dt + 1) * 128],
                        iden_f[:]),
                        inc="xtp",
                        waits=[("xin", tin), bw(6)] + list(xT_free))
                    tch_last = ttp
                    em("v", lambda v, dt=dt, tch=tch: v.tensor_copy(
                        xT[:, dt * T + tch * 128: dt * T + (tch + 1) * 128],
                        PS[6][0:128, 0:128]),
                        inc=br(6), waits=[("xtp", ttp)])
            return ("pb6", bn(6))

        def layer_norm(ci, x_toks, yln_free):
            sv = "lnv"
            ss = "lns_"
            st = "lnt_"
            done = 0
            t4_prev = 0
            t4b_prev = 0
            t_exp_prev = 0
            t3_prev = 0
            for ch in range(NTC):
                cs = ch * 512
                mmq_t = []
                t_mean = None
                for dt in range(NDT):
                    w = list(x_toks) + ([("pb3x", mmq_t[dt - 2])] if dt >= 2 else [])
                    tq = em("s", lambda s, dt=dt, cs=cs: s.activation(
                        xsq[:, (dt % 2) * 512:(dt % 2) * 512 + 512],
                        xT[:, dt * T + cs: dt * T + cs + 512], AF.Square),
                        inc=ss, waits=w)
                    t_mean = MM(meanw[:], xT[:, dt * T + cs: dt * T + cs + 512], 2,
                                PS[2][0:1, 0:512], start=(dt == 0),
                                stop=(dt == NDT - 1),
                                inc=(st if dt == NDT - 1 else None),
                                waits=(list(x_toks) if dt == 0 else []))
                    tmm = MM(meanw[:], xsq[:, (dt % 2) * 512:(dt % 2) * 512 + 512], 3,
                             PS[3][0:1, 0:512], start=(dt == 0), stop=(dt == NDT - 1),
                             inc="pb3x", waits=[(ss, tq)])
                    mmq_t.append(tmm)
                t1 = em("v", lambda v: v.tensor_copy(mrow[:], PS[2][0:1, 0:512]),
                        inc=br(2), waits=[(st, t_mean)])
                t1b = em("v", lambda v: v.tensor_copy(srow[:], PS[3][0:1, 0:512]),
                        inc=br(3), waits=[("pb3x", mmq_t[-1]), ("pb2", t1),
                                          (ss, t_exp_prev)])
                t1 = em("v", lambda v: v.tensor_tensor(arow[:], mrow[:], mrow[:],
                                                       ALU.mult),
                        inc=sv, waits=[("pb3", t1b)])
                t1 = em("v", lambda v: v.tensor_sub(srow[:], srow[:], arow[:]),
                        inc=sv, waits=[(sv, t1)])
                t2 = em("s", lambda s: s.activation(srow[:], srow[:], AF.Ln,
                                                    bias=float(EPS_LN)),
                        inc=ss, waits=[(sv, t1)])
                t2 = em("s", lambda s: s.activation(arow[:], srow[:], AF.Exp,
                                                    scale=-0.5),
                        inc=ss, waits=[(ss, t2), (st, t4_prev), (sv, t3_prev)])
                t_exp_prev = t2
                t3 = em("v", lambda v: v.scalar_tensor_tensor(
                    brow[:], mrow[:], -1.0, arow[:], ALU.mult, ALU.mult),
                    inc=sv, waits=[(ss, t2), (st, t4b_prev)])
                t3_prev = t3
                t4 = MM(ones_r[:], arow[:], 4, PS[4][:, 0:512], True, True,
                        inc=st, waits=[(ss, t2)])
                t4_prev = t4
                t4b = MM(ones_r[:], brow[:], 5, PS[5][:, 0:512], True, True,
                         inc=st, waits=[(sv, t3)])
                t4b_prev = t4b
                for dt in range(NDT):
                    w5 = [(st, t4b), (st, t4)]
                    if dt < 2:
                        w5 += list(yln_free)
                    t5 = em("v", lambda v, dt=dt, cs=cs: v.tensor_tensor(
                        wgtbc[:, (dt % 2) * 512:(dt % 2) * 512 + 512],
                        xT[:, dt * T + cs: dt * T + cs + 512], PS[4][:, 0:512],
                        ALU.mult),
                        inc=br(4), waits=w5)
                    done = em("v", lambda v, dt=dt, cs=cs: v.tensor_tensor(
                        ylnT[:, dt * T + cs: dt * T + cs + 512],
                        wgtbc[:, (dt % 2) * 512:(dt % 2) * 512 + 512], PS[5][:, 0:512],
                        ALU.add),
                        inc=br(5), waits=[("pb4", t5)])
            return ("pb5", bn(5))

        def routing(ci, yln, toks):
            sv = "rtv"
            st = "rtt_"
            t_dma = 0
            t_tp_prev = 0
            t_exp_prev = 0
            for tch in range(NCH):
                for dt in range(NDT):
                    MM(ylnT[:, dt * T + tch * 128: dt * T + (tch + 1) * 128],
                       wrouteT[:, dt * 16:(dt + 1) * 16], 4, PS[4][0:128, 0:16],
                       start=(dt == 0), stop=False,
                       waits=([yln, ("pb7", toks["wrouteT"])] if dt == 0 else []))
                t_lg = MM(ones_r[:], brtrow[:], 4, PS[4][0:128, 0:16],
                          start=False, stop=True, inc=st,
                          waits=[("pb0", toks["broute"])])
                t1 = em("v", lambda v: v.tensor_copy(Lsb[:], PS[4][0:128, 0:16]),
                        inc=br(4), waits=[(st, t_lg)])
                t1 = em("v", lambda v: v.tensor_copy(Lwk[:], Lsb[:]),
                        inc=sv, waits=[("pb4", t1), ("rts_", t_exp_prev),
                                       (st, t_tp_prev)])
                for i in range(4):
                    t1 = em("v", lambda v, i=i: v.reduce_max(
                        Mtop[:, i:i + 1], Lwk[:], axis=AX),
                        inc=sv, waits=[(sv, t1)])
                    t1 = em("v", lambda v, i=i: v.tensor_scalar(
                        masks[:, 16 * i:16 * (i + 1)], Lsb[:], Mtop[:, i:i + 1],
                        None, ALU.is_equal),
                        inc=sv, waits=[(sv, t1)])
                    if i < 3:
                        t1 = em("v", lambda v, i=i: v.scalar_tensor_tensor(
                            Lwk[:], masks[:, 16 * i:16 * (i + 1)], -1.0e30, Lwk[:],
                            ALU.mult, ALU.add),
                            inc=sv, waits=[(sv, t1)])
                t1 = em("v", lambda v: v.tensor_scalar_mul(negm1[:], Mtop[:, 0:1], -1.0),
                        inc=sv, waits=[(sv, t1)])
                t2 = em("s", lambda s: s.activation(Etop[:], Mtop[:], AF.Exp,
                                                    bias=negm1[:]),
                        inc="rts_", waits=[(sv, t1)])
                t_exp_prev = t2
                t3 = em("v", lambda v: v.reduce_sum(dsum[:], Etop[:], axis=AX),
                        inc=sv, waits=[("rts_", t2)])
                t3 = em("v", lambda v: v.reciprocal(winv[:], dsum[:]),
                        inc=sv, waits=[(sv, t3)])
                t3 = em("v", lambda v: v.tensor_scalar(Ptop[:], Etop[:], winv[:], 0.5,
                                                       ALU.mult, ALU.mult),
                        inc=sv, waits=[(sv, t3)])
                t3 = em("v", lambda v: v.tensor_scalar_mul(
                    wgt_tok[:], masks[:, 0:16], Ptop[:, 0:1]),
                    inc=sv, waits=[(sv, t3)])
                for i in range(1, 4):
                    t3 = em("v", lambda v, i=i: v.scalar_tensor_tensor(
                        wgt_tok[:], masks[:, 16 * i:16 * (i + 1)], Ptop[:, i:i + 1],
                        wgt_tok[:], ALU.mult, ALU.add),
                        inc=sv, waits=[(sv, t3)])
                t4 = em("t", lambda t: t.transpose(PS[6][0:16, 0:128], wgt_tok[:],
                                                   iden_f[:]),
                        inc=st, waits=[(sv, t3), bw(6), ("std", snow("std"))])
                t_tp_prev = t4
                t5 = em("v", lambda v: v.tensor_copy(wgt16[:], PS[6][0:16, 0:128]),
                        inc=br(6), waits=[(st, t4), ("rtd_", t_dma)])
                t_dma = em("g", lambda g, ci=ci, tch=tch: g.dma_start(
                    wgtd[ci][:, tch * 128:(tch + 1) * 128], wgt16[:]),
                    inc="rtd_", inc_by=16, waits=[("pb6", t5)])
            return ("rtd_", t_dma)

        def ffn(ci, yln, toks, pool_free, wstream_free):
            rt = routing(ci, yln, toks)
            dump(f"yln{ci}", ylnT[:, 0:NDT * T], [128, NDT * T], BF16, waits=[yln])
            dump(f"wgt{ci}", wgtd[ci], [E, T], F32, waits=[rt])
            sh = "fh_"
            
            so = "fo_"
            sw1 = "fw1"
            sw2 = "fw2"
            sev2 = "fe2_"

            sbb = "fbb_"
            for ch in range(NTC):
                t0 = em("g", lambda g, ci=ci, ch=ch: g.dma_start(
                    wgt16b[:], wgtd[ci][:, ch * 512:(ch + 1) * 512]),
                    inc="fbl_", inc_by=16, waits=[rt, (sbb, snow(sbb))])
                for dot in range(NDT):
                    t1 = MM(b2nat[:, dot * 128:(dot + 1) * 128], wgt16b[:], 6,
                            PS[6][0:128, 0:512], True, True, inc=sbb,
                            waits=[("fbl_", t0), ("ld", toks["ld"]), yln])
                    em("v", lambda v, dot=dot, ch=ch: v.tensor_add(
                        xT[:, dot * T + ch * 512: dot * T + ch * 512 + 512],
                        xT[:, dot * T + ch * 512: dot * T + ch * 512 + 512],
                        PS[6][0:128, 0:512]),
                       inc=br(6), waits=[(sbb, t1)])
            bias_done = ("pb6", bn(6))
            dump(f"xb{ci}", xT[:, 0:NDT * T], [128, NDT * T], F32,
                 waits=[bias_done])

            w1_idx0 = ffn_st["w1_idx"]
            slot0 = len(ffn_st["slots"])
            mm1_of_slab = ffn_st["slabs"]
            mm2_of_slot = ffn_st["slots"]
            wrow_slot_mm = ffn_st["wrow"]
            for e in range(cfg.get("NE", E)):
                
                for ch in range(NTC):
                    fwrp = f"fwr_{ch % 2}"
                    t0 = em("g", lambda g, ci=ci, e=e, ch=ch: g.dma_start(
                        wgtrow[0:1, (ch % 2) * 512:(ch % 2) * 512 + 512],
                        wgtd[ci][e:e + 1, ch * 512:(ch + 1) * 512]),
                        inc=fwrp, inc_by=16,
                        waits=[rt, ("fwb_", wrow_slot_mm[ch % 2])])
                    t1 = MM(ones_r[:], wgtrow[0:1, (ch % 2) * 512:(ch % 2) * 512 + 512],
                            6, PS[6][0:128, 0:512], True, True, inc="fwb_",
                            waits=[(fwrp, t0)])
                    wrow_slot_mm[ch % 2] = t1
                    em("v", lambda v, ch=ch: v.tensor_copy(
                        wgtbc[:, ch * 512:(ch + 1) * 512], PS[6][0:128, 0:512]),
                       inc=br(6), waits=[("fwb_", t1), (sev2, snow(sev2))])
                wgt_ready = bn(6)
                if e < 2:
                    dump(f"wbc{ci}_{e}", wgtbc[:, 0:NTC * 512], [128, NTC * 512],
                         F32, waits=[("pb6", wgt_ready)])

                for fq in range(4):
                    slot = slot0 + e * 4 + fq
                    p2 = slot % 2
                    sw2p = f"{sw2}_{p2}"
                    w2w = [(f"cast{e}", 32)]
                    if slot >= 2:
                        w2w.append((so, mm2_of_slot[slot - 2]))
                    if e == 0 and fq == 0:
                        w2w += list(pool_free) + list(wstream_free)
                    for ftl in range(8):
                        em("y", lambda y, e=e, fq=fq, ftl=ftl, p2=p2: y.dma_start(
                            poolA[:, p2 * 8192 + ftl * 1024:
                                  p2 * 8192 + (ftl + 1) * 1024],
                            w2b[e, 0:1024,
                                (fq * 8 + ftl) * 128:(fq * 8 + ftl + 1) * 128],
                            transpose=True),
                           inc=sw2p, inc_by=16, waits=(w2w if ftl == 0 else []))
                    w2_tok = (sw2p, snow(sw2p))

                    for ch in range(NTC):
                        cs = ch * 512
                        gel_banks = []
                        for half in range(2):
                            w1_idx = ffn_st["w1_idx"]
                            p1 = w1_idx % 2
                            sw1p = f"{sw1}_{p1}"
                            w1w = [(f"cast{e}", 32)]
                            if w1_idx >= 2:
                                w1w.append((sh, mm1_of_slab[w1_idx - 2]))
                            if w1_idx == w1_idx0 and w1_idx0 == 0:
                                w1w += list(pool_free) + list(wstream_free)
                            if w1_idx == w1_idx0 and w1_idx0 > 0:
                                w1w += list(pool_free) + list(wstream_free)
                            for dt in range(NDT):
                                em("y", lambda y, e=e, fq=fq, half=half, dt=dt, p1=p1:
                                   y.dma_start(
                                    poolC[:, p1 * 4096 + dt * 512:
                                          p1 * 4096 + (dt + 1) * 512],
                                    w1b[e, (fq * 8 + half * 4) * 128:
                                        (fq * 8 + half * 4 + 4) * 128,
                                        dt * 128:(dt + 1) * 128],
                                    transpose=True),
                                   inc=sw1p, inc_by=16,
                                   waits=(w1w if dt == 0 else []))
                            w1_tok = (sw1p, snow(sw1p))
                            for ftl2 in range(4):
                                ft = half * 4 + ftl2
                                ftg = fq * 8 + ft
                                hp = ft % 2
                                for dt in range(NDT):
                                    MM(poolC[:, p1 * 4096 + dt * 512 + ftl2 * 128:
                                             p1 * 4096 + dt * 512 + (ftl2 + 1) * 128],
                                       ylnT[:, dt * T + cs: dt * T + cs + 512],
                                       hp, PS[hp][0:128, 0:512],
                                       start=(dt == 0), stop=(dt == NDT - 1),
                                       inc=(sh if dt == NDT - 1 else None),
                                       waits=([w1_tok, yln] if dt == 0 else []))
                                t_h = snow(sh)
                                gel_banks.append(
                                    em("s", lambda s, ft=ft, hp=hp, e=e, ftg=ftg:
                                       s.activation(
                                        poolB[:, ft * 512:(ft + 1) * 512],
                                        PS[hp][0:128, 0:512], AF.Gelu,
                                        bias=b1colT[:, ftg * 16 + e:
                                                    ftg * 16 + e + 1]),
                                       inc=br(hp),
                                       waits=[(sh, t_h), (so, snow(so)),
                                              ("pb1", toks["bias"])]))
                            mm1_of_slab.append(snow(sh))
                            ffn_st["w1_idx"] += 1
                        g0 = bn(0)
                        g1 = bn(1)
                        for dot in range(NDT):
                            op = 2 + (dot % 2)
                            for ftl3 in range(8):
                                MM(poolA[:, p2 * 8192 + ftl3 * 1024 + dot * 128:
                                         p2 * 8192 + ftl3 * 1024 + (dot + 1) * 128],
                                   poolB[:, ftl3 * 512:(ftl3 + 1) * 512],
                                   op, PS[op][0:128, 0:512],
                                   start=(ftl3 == 0), stop=(ftl3 == 7),
                                   inc=(so if ftl3 == 7 else None),
                                   waits=([("pb0", g0), ("pb1", g1),
                                           w2_tok] if ftl3 == 0 else []))
                            t_o = snow(so)
                            t_p1 = em("v", lambda v, op=op, ch=ch, dot=dot:
                                      v.tensor_tensor(
                                       xsq[:, (dot % 2) * 512:(dot % 2) * 512 + 512],
                                       PS[op][0:128, 0:512],
                                       wgtbc[:, ch * 512:(ch + 1) * 512], ALU.mult),
                                      inc=br(op),
                                      waits=[(so, t_o), ("pb6", wgt_ready),
                                             (sev2, max(0, snow(sev2) - 1))])
                            em("v", lambda v, dot=dot, ch=ch: v.tensor_add(
                                xT[:, dot * T + ch * 512: dot * T + ch * 512 + 512],
                                xT[:, dot * T + ch * 512: dot * T + ch * 512 + 512],
                                xsq[:, (dot % 2) * 512:(dot % 2) * 512 + 512]),
                               inc=sev2,
                               waits=[(f"pb{op}", t_p1), bias_done])
                    mm2_of_slot.append(snow(so))
            return [(sev2, snow(sev2)), ("pb6", bn(6))], (so, snow(so))

        def attn(ab, yln, toks, pool_free, wstream_free):
            sv = "av_"
            st = "at_"
            ss = "as_"
            sd = "ad_"
            sx = "ax_"
            QB, KB, QF, KF, ATM, WTB = 0, T, 2 * T, 3 * T, 4 * T, 4 * T + 128
            KTK, QSQ = 0, 2048
            VOT = 0

            qkv_mm_done = {}
            votok3 = poolC[:, VOT:VOT + NCH * 130].rearrange(
                "p (n c) -> p n c", c=130)
            t_on = em("v", lambda v: v.memset(votok3[:, :, 64:65], 1.0),
                      inc=sv, waits=list(pool_free))
            t_on = em("v", lambda v: v.memset(votok3[:, :, 129:130], 1.0), inc=sv)

            for dqt in range(NDT):
                pw = dqt % 2
                wtb = WTB + pw * 3072
                sdp = f"{sd}_{pw}"
                for mi, mat in enumerate((wqb, wkb, wvb)):
                    for dt in range(NDT):
                        w = [("castA", 64)]
                        if dqt == 0 and mi == 0 and dt == 0:
                            w += list(pool_free) + list(wstream_free)
                        if dqt >= 1 and mi == 0 and dt == 0:
                            w.append((st, qkv_mm_done[dqt - 1]))
                        em("y", lambda y, m=mat, dqt=dqt, dt=dt, mi=mi, wtb=wtb:
                           y.dma_start(
                            poolA[:, wtb + mi * 1024 + dt * 128:
                                  wtb + mi * 1024 + (dt + 1) * 128],
                            m[dqt * 128:(dqt + 1) * 128, dt * 128:(dt + 1) * 128],
                            transpose=True),
                           inc=sdp, inc_by=16, waits=w)
                wv_full = (sdp, snow(sdp))
                wv_toks = {0: wv_full, 1: wv_full, 2: wv_full}
                t_wo = em("y", lambda y, dqt=dqt: y.dma_start(
                    wotb[:], wob[0:1024, dqt * 128:(dqt + 1) * 128], transpose=True),
                    inc=f"{sd}wo", inc_by=16,
                    waits=[("castA", 64), (st, snow(st))])

                qkev_tok = ("pb0", 0)
                for mi, (dst, bcol) in enumerate(((QB, bqcol), (KB, bkcol))):
                    for ch in range(NTC):
                        up = ch % 2
                        for dt in range(NDT):
                            MM(poolA[:, wtb + mi * 1024 + dt * 128:
                                     wtb + mi * 1024 + (dt + 1) * 128],
                               ylnT[:, dt * T + ch * 512: dt * T + ch * 512 + 512],
                               up, PS[up][0:128, 0:512],
                               start=(dt == 0), stop=(dt == NDT - 1),
                               inc=(st if dt == NDT - 1 else None),
                               waits=([wv_toks[mi], yln] if dt == 0 else []))
                        t_qk = snow(st)
                        _sem = br(up)
                        _t = em("s", lambda s, dst=dst, ch=ch, up=up, dqt=dqt,
                                bcol=bcol: s.activation(
                                 poolA[:, dst + ch * 512: dst + ch * 512 + 512],
                                 PS[up][0:128, 0:512], AF.Identity,
                                 bias=bcol[:, dqt:dqt + 1]),
                                inc=_sem, waits=[(st, t_qk)])
                        qkev_tok = (_sem, _t)

                for tch in range(NCH):
                    vp_ = 2 + (tch % 2)
                    for dt in range(NDT):
                        MM(ylnT[:, dt * T + tch * 128: dt * T + (tch + 1) * 128],
                           poolA[:, wtb + 2 * 1024 + dt * 128:
                                 wtb + 2 * 1024 + (dt + 1) * 128],
                           vp_, PS[vp_][0:128, 0:128],
                           start=(dt == 0), stop=False,
                           waits=([wv_toks[2], yln] if dt == 0 else []))
                    t_v = MM(ones_r[:], bvrow[0:1, dqt * 128:(dqt + 1) * 128],
                             vp_, PS[vp_][0:128, 0:128], False, True, inc=st,
                             waits=[("ld", toks["bvrow"])])
                    for hh2 in range(2):
                        em("v", lambda v, tch=tch, hh2=hh2, vp_=vp_: v.tensor_copy(
                            poolC[:, VOT + tch * 130 + hh2 * 65:
                                  VOT + tch * 130 + hh2 * 65 + 64],
                            PS[vp_][0:128, hh2 * 64: hh2 * 64 + 64]),
                           inc=br(vp_), waits=[(st, t_v), (sv, t_on)])
                t_vev = (bn(2), bn(3))

                for hh in range(2):
                    hs = hh * 64
                    for (srcb, dstb) in ((QB, QF), (KB, KF)):
                        t_sq = em("s", lambda s, hs=hs, srcb=srcb: s.activation(
                            poolB[hs:hs + 64, QSQ:QSQ + T],
                            poolA[hs:hs + 64, srcb:srcb + T], AF.Square),
                            inc=ss, waits=[qkev_tok, ("pb7", bn(7)),
                                           (st, snow(st))])
                        for ch in range(NTC):
                            up = ch % 2
                            t_s1 = MM(halfsq_b[hs:hs + 64, :],
                                      poolB[hs:hs + 64,
                                            QSQ + ch * 512: QSQ + ch * 512 + 512],
                                      4, PS[4][0:1, 0:512], True, True, inc=st,
                                      waits=[(ss, t_sq)])
                            t_s2 = em("v", lambda v, ch=ch: v.tensor_copy(
                                sqrow[0:1, (ch % 2) * 512:(ch % 2) * 512 + 512],
                                PS[4][0:1, 0:512]),
                                inc=br(4), waits=[(st, t_s1), (st, snow(st))])
                            MM(projT[hs:hs + 64, :],
                               poolA[hs:hs + 64, srcb + ch * 512:
                                     srcb + ch * 512 + 512],
                               up, PS[up][0:128, 0:512], True, False,
                               waits=[("pb7", toks["projT"]), qkev_tok])
                            t_u = MM(negones_r[:],
                                     sqrow[0:1, (ch % 2) * 512:(ch % 2) * 512 + 512],
                                     up, PS[up][0:128, 0:512], False, True, inc=st,
                                     waits=[("pb4", t_s2)])
                            t_e = em("s", lambda s, dstb=dstb, ch=ch, up=up:
                                     s.activation(
                                      poolA[:, dstb + ch * 512: dstb + ch * 512 + 512],
                                      PS[up][0:128, 0:512], AF.Exp,
                                      bias=-float(LN_SQRT_M)),
                                     inc=br(up), waits=[(st, t_u)])
                            em("v", lambda v, dstb=dstb, ch=ch, up=up:
                               v.tensor_scalar_add(
                                poolA[:, dstb + ch * 512: dstb + ch * 512 + 512],
                                poolA[:, dstb + ch * 512: dstb + ch * 512 + 512],
                                float(EPS_ATT)),
                               inc=sv, waits=[(f"pb{up}", t_e)])
                    t_phi = snow(sv)

                    for n in range(NCH):
                        t_t1 = em("t", lambda t, n=n: t.transpose(
                            PT[0:128, 0:128],
                            poolA[:, KF + n * 128: KF + (n + 1) * 128], iden_b[:]),
                            inc=st, waits=[(sv, t_phi), bw(7)])
                        em("v", lambda v, n=n: v.tensor_copy(
                            poolB[:, KTK + n * 128: KTK + (n + 1) * 128],
                            PT[0:128, 0:128]),
                           inc=br(7), waits=[(st, t_t1)])
                    t_ktk = bn(7)

                    rb_prev = 0
                    for n in range(NCH):
                        vo65 = VOT + n * 130 + hh * 65
                        if n > 0:
                            t_gc = em("v", lambda v: v.tensor_copy(
                                Gsb[:], PS[5][0:128, 0:65]),
                                inc=br(5), waits=[(st, snow(st))])
                        else:
                            t_gc = 0
                        MM(poolB[:, KTK + n * 128: KTK + (n + 1) * 128],
                           poolC[:, vo65:vo65 + 65], 5, PS[5][0:128, 0:65],
                           start=(n == 0), stop=(n == NCH - 1), inc=st,
                           waits=[("pb7", t_ktk), ("pb2", t_vev[0]),
                                  ("pb3", t_vev[1])] +
                                 ([("pb5", t_gc)] if n > 0 else []))
                        t_at = MM(poolA[:, KF + n * 128: KF + (n + 1) * 128],
                                  poolA[:, QF + n * 128: QF + (n + 1) * 128],
                                  4, PS[4][0:128, 0:128], True, True, inc=st,
                                  waits=[(sv, t_phi)])
                        t_atm = em("v", lambda v: v.tensor_tensor(
                            poolA[:, ATM:ATM + 128], PS[4][0:128, 0:128], triu_b[:],
                            ALU.mult),
                            inc=br(4), waits=[(st, t_at)])
                        if n > 0:
                            MM(Gsb[:, 0:64],
                               poolA[:, QF + n * 128: QF + (n + 1) * 128],
                               2, PS[2][hs:hs + 64, 0:128], True, False,
                               waits=[("pb5", t_gc), ("pb4", t_atm)])
                            t_num = MM(poolC[:, vo65:vo65 + 64],
                                       poolA[:, ATM:ATM + 128],
                                       2, PS[2][hs:hs + 64, 0:128], False, True,
                                       inc=st, waits=[])
                            MM(Gsb[:, 64:65],
                               poolA[:, QF + n * 128: QF + (n + 1) * 128],
                               3, PS[3][0:1, 0:128], True, False, waits=[])
                            t_den = MM(poolC[:, vo65 + 64:vo65 + 65],
                                       poolA[:, ATM:ATM + 128],
                                       3, PS[3][0:1, 0:128], False, True,
                                       inc=st, waits=[])
                        else:
                            t_num = MM(poolC[:, vo65:vo65 + 64],
                                       poolA[:, ATM:ATM + 128],
                                       2, PS[2][hs:hs + 64, 0:128], True, True,
                                       inc=st, waits=[("pb4", t_atm)])
                            t_den = MM(poolC[:, vo65 + 64:vo65 + 65],
                                       poolA[:, ATM:ATM + 128],
                                       3, PS[3][0:1, 0:128], True, True,
                                       inc=st, waits=[])
                        t_d1 = em("v", lambda v: v.tensor_scalar_add(
                            denrow[:], PS[3][0:1, 0:128], float(EPS_ATT)),
                            inc=br(3), waits=[(st, t_den)])
                        t_d2 = em("v", lambda v: v.reciprocal(rcprow[:], denrow[:]),
                                  inc=sv, waits=[("pb3", t_d1), (st, rb_prev)])
                        t_rb = MM(ones_r[0:1, 0:64], rcprow[:], 4,
                                  PS[4][hs:hs + 64, 128:256], True, True, inc=st,
                                  waits=[(sv, t_d2)])
                        rb_prev = t_rb
                        t_rbe = em("v", lambda v, hs=hs: v.tensor_copy(
                            rbsb[hs:hs + 64, :], PS[4][hs:hs + 64, 128:256]),
                            inc=br(4), waits=[(st, t_rb)])
                        em("v", lambda v, hs=hs, n=n: v.tensor_tensor(
                            attn_dqT[hs:hs + 64, n * 128:(n + 1) * 128],
                            PS[2][hs:hs + 64, 0:128], rbsb[hs:hs + 64, :],
                            ALU.mult),
                           inc=br(2), waits=[(st, t_num), ("pb4", t_rbe)])
                t_attn_out = bn(2)

                for dot in range(NDT):
                    for ch in range(NTC):
                        up = ch % 2
                        t_op = MM(wotb[:, dot * 128:(dot + 1) * 128],
                                  attn_dqT[:, ch * 512: ch * 512 + 512],
                                  up, PS[up][0:128, 0:512], True, True, inc=st,
                                  waits=[("pb2", t_attn_out), (f"{sd}wo", t_wo)])
                        if dqt == 0:
                            em("v", lambda v, dot=dot, ch=ch, up=up:
                               v.scalar_tensor_tensor(
                                xT[:, dot * T + ch * 512: dot * T + ch * 512 + 512],
                                PS[up][0:128, 0:512], bocol[:, dot:dot + 1],
                                xT[:, dot * T + ch * 512: dot * T + ch * 512 + 512],
                                ALU.add, ALU.add),
                               inc=br(up), waits=[(st, t_op), ("pb1", toks["bias"])])
                        else:
                            em("v", lambda v, dot=dot, ch=ch, up=up: v.tensor_add(
                                xT[:, dot * T + ch * 512: dot * T + ch * 512 + 512],
                                xT[:, dot * T + ch * 512: dot * T + ch * 512 + 512],
                                PS[up][0:128, 0:512]),
                               inc=br(up), waits=[(st, t_op)])
                qkv_mm_done[dqt] = snow(st)
            return [("pb0", bn(0)), ("pb1", bn(1))], (st, snow(st))

        def store_x(b, x_toks):
            last = 0
            for tch in range(T // 128):
                ev = 0
                for dt in range(NDT):
                    t1 = em("t", lambda t, dt=dt, tch=tch: t.transpose(
                        PS[6][0:128, 0:128],
                        xT[:, dt * T + tch * 128: dt * T + (tch + 1) * 128],
                        iden_f[:]),
                        inc="xtp",
                        waits=list(x_toks) + [bw(6), ("std", snow("std"))])
                    ev = em("v", lambda v, dt=dt: v.tensor_copy(
                        stage[:, dt * 128:(dt + 1) * 128], PS[6][0:128, 0:128]),
                        inc=br(6), waits=[("xtp", t1), ("std", snow("std"))])
                last = em("g", lambda g, b=b, tch=tch: g.dma_start(
                    yout[b, tch * 128:(tch + 1) * 128, :], stage[:]),
                    inc="std", inc_by=16, waits=[("pb6", ev)])
            return ("std", last)

        toks = setup()
        store_tok = None
        prev_tensor_done = []
        prev_scalar_banks = []
        for b in range(NB):
            if b == 0:
                stage_free = list(toks["stage_users"])
                xT_free = []
            else:
                stage_free = [store_tok]
                xT_free = [("xtp", snow("xtp"))]
            x_toks = [load_x(b, stage_free, xT_free)]
            yln_free = prev_tensor_done
            for ph in PHASES:
                if ph == "ffn1":
                    ci = 2 * b
                    yln = layer_norm(f"a{ci}", x_toks, yln_free)
                    x_toks, tdone = ffn(ci, yln, toks, prev_tensor_done,
                                        prev_tensor_done)
                    prev_tensor_done = [tdone]
                    yln_free = [tdone]
                elif ph == "attn":
                    yln = layer_norm(f"b{b}", x_toks, yln_free)
                    dump(f"yln_attn{b}", ylnT[:, 0:NDT * T], [128, NDT * T], BF16,
                         waits=[yln])
                    x_toks, tdone = attn(b, yln, toks, prev_tensor_done,
                                         prev_tensor_done)
                    prev_tensor_done = [tdone]
                    yln_free = [tdone]
                elif ph == "ffn2":
                    ci = 2 * b + 1
                    yln = layer_norm(f"c{ci}", x_toks, yln_free)
                    x_toks, tdone = ffn(ci, yln, toks, prev_tensor_done,
                                        prev_tensor_done)
                    prev_tensor_done = [tdone]
                    yln_free = [tdone]
            store_tok = store_x(b, x_toks)
        em("g", lambda g: g.engine_nop(), waits=[store_tok])

        with nc.Block() as blk:
            @blk.tensor
            def _(h):
                for f_ in ops["t"]:
                    f_(h)

            @blk.scalar
            def _(h):
                for f_ in ops["s"]:
                    f_(h)

            @blk.vector
            def _(h):
                for f_ in ops["v"]:
                    f_(h)

            @blk.gpsimd
            def _(h):
                for f_ in ops["g"]:
                    f_(h)

            @blk.sync
            def _(h):
                for f_ in ops["y"]:
                    f_(h)

    return nc, dbg_outs


def kernel(**inputs):
    cfg = dict(CFG)
    nc, dbg = build_program(cfg)
    NB, T, NC = cfg["NB"], cfg["T"], cfg["NCORES"]
    x = np.ascontiguousarray(np.asarray(inputs["L0_attn_x"], dtype=np.float32))
    in_maps = []
    for c in range(NC):
        m = {k: np.ascontiguousarray(np.asarray(v, dtype=np.float32))
             for k, v in inputs.items() if k != "L0_attn_x"}
        m["L0_attn_x"] = np.ascontiguousarray(x[c * NB:(c + 1) * NB, :T])
        in_maps.append(m)
    res = run_bass_kernel_spmd(nc, in_maps, core_ids=list(range(NC)))
    out = np.concatenate([r["y"] for r in res.results], axis=0)
    return out


# version 04 (final)
# speedup vs baseline: 1.0602x; 1.0602x over previous
import math
import os
from contextlib import ExitStack

os.environ.setdefault("GAUGE_NTFF_IGNORE_DMA", "1")

import numpy as np
import ml_dtypes
import concourse.bass as bass
import concourse.mybir as mybir
from concourse.bass_utils import run_bass_kernel_spmd

F32 = mybir.dt.float32
BF16 = mybir.dt.bfloat16
AF = mybir.ActivationFunctionType
ALU = mybir.AluOpType
AX = mybir.AxisListType.X

CFG = dict(
    T=2048,
    NB=2,
    NCORES=8,
    PHASES=("ffn1", "attn", "ffn2"),
    DEBUG=(),
    NE=16,
)

D = 1024
NDT = 8
F = 4096
E = 16
DKH = 64
PM = 128
ACH = 128
EPS_ATT = 1e-6
EPS_LN = 1e-5
LN_SQRT_M = 0.5 * math.log(PM)
INV_DK4 = 1.0 / (DKH ** 0.25)


def build_program(cfg=None):
    cfg = dict(CFG if cfg is None else cfg)
    T = cfg["T"]
    NB = cfg["NB"]
    NTC = T // 512
    NCH = T // ACH
    PHASES = cfg["PHASES"]
    DEBUG = set(cfg["DEBUG"])

    nc = bass.Bass("TRN2", target_bir_lowering=False)

    xin = nc.dram_tensor("L0_attn_x", [NB, T, D], F32, kind="ExternalInput").ap()
    wq_d = nc.dram_tensor("L0_attn_Wq", [D, D], F32, kind="ExternalInput").ap()
    bq_d = nc.dram_tensor("L0_attn_bq", [D], F32, kind="ExternalInput").ap()
    wk_d = nc.dram_tensor("L0_attn_Wk", [D, D], F32, kind="ExternalInput").ap()
    bk_d = nc.dram_tensor("L0_attn_bk", [D], F32, kind="ExternalInput").ap()
    wv_d = nc.dram_tensor("L0_attn_Wv", [D, D], F32, kind="ExternalInput").ap()
    bv_d = nc.dram_tensor("L0_attn_bv", [D], F32, kind="ExternalInput").ap()
    wo_d = nc.dram_tensor("L0_attn_Wo", [D, D], F32, kind="ExternalInput").ap()
    bo_d = nc.dram_tensor("L0_attn_bo", [D], F32, kind="ExternalInput").ap()
    proj_d = nc.dram_tensor("L0_attn_proj", [PM, DKH], F32, kind="ExternalInput").ap()
    keys_d = nc.dram_tensor("L0_ffn_keys", [E, D], F32, kind="ExternalInput").ap()
    w1_d = nc.dram_tensor("L0_ffn_w1", [E, F, D], F32, kind="ExternalInput").ap()
    b1_d = nc.dram_tensor("L0_ffn_b1", [E, F], F32, kind="ExternalInput").ap()
    w2_d = nc.dram_tensor("L0_ffn_w2", [E, D, F], F32, kind="ExternalInput").ap()
    b2_d = nc.dram_tensor("L0_ffn_b2", [E, D], F32, kind="ExternalInput").ap()
    rw_d = nc.dram_tensor("L0_ffn_rw", [E, D], F32, kind="ExternalInput").ap()
    rb_d = nc.dram_tensor("L0_ffn_rb", [E], F32, kind="ExternalInput").ap()
    yout = nc.dram_tensor("y", [NB, T, D], F32, kind="ExternalOutput").ap()

    w1b = nc.dram_tensor("w1b", [E, F, D], BF16, kind="Internal").ap()
    w2b = nc.dram_tensor("w2b", [E, D, F], BF16, kind="Internal").ap()
    wqb = nc.dram_tensor("wqb", [D, D], BF16, kind="Internal").ap()
    wkb = nc.dram_tensor("wkb", [D, D], BF16, kind="Internal").ap()
    wvb = nc.dram_tensor("wvb", [D, D], BF16, kind="Internal").ap()
    wob = nc.dram_tensor("wob", [D, D], BF16, kind="Internal").ap()
    wgtd = nc.dram_tensor("wgtd", [2 * NB, E, T], F32, kind="Internal").ap()
    w1t = nc.dram_tensor("w1t", [E, D, F], BF16, kind="Internal").ap()
    w2t = nc.dram_tensor("w2t", [E, F, D], BF16, kind="Internal").ap()

    ctx = ExitStack()

    def sbt(name, shape, dt=F32):
        return ctx.enter_context(nc.sbuf_tensor(name, shape, dt))

    def psb(name, shape, dt=F32):
        return ctx.enter_context(nc.psum_tensor(name, shape, dt))

    dbg_outs = {}

    with ctx:
        consts = sbt("consts", [128, 8])
        xT = sbt("xT", [128, NDT * T])
        ylnT = sbt("ylnT", [128, NDT * T], BF16)
        poolA = sbt("poolA", [128, 16384], BF16)
        poolB = sbt("poolB", [128, 4096], BF16)
        poolC = sbt("poolC", [128, 8192], BF16)
        attn_dqT = sbt("attn_dqT", [128, max(T, 2048)], BF16)
        wotb = sbt("wotb", [128, 1024], BF16)
        stage = sbt("stage", [128, 1024])
        xsq = sbt("xsq", [128, 1024])
        wgtbc = sbt("wgtbc", [128, 2048])
        iden_f = sbt("iden_f", [128, 128])
        iden_b = sbt("iden_b", [128, 128], BF16)
        triu_b = sbt("triu_b", [128, 128], BF16)
        ones_r = sbt("ones_r", [1, 128])
        negones_r = sbt("negones_r", [1, 128])
        ones_1 = sbt("ones_1", [1, 8])
        meanw = sbt("meanw", [128, 1])
        halfsq_b = sbt("halfsq_b", [128, 1], BF16)
        projT = sbt("projT", [128, 128], BF16)
        bqcol = sbt("bqcol", [128, 8])
        bkcol = sbt("bkcol", [128, 8])
        bocol = sbt("bocol", [128, 8])
        bvrow = sbt("bvrow", [1, 1024])
        b1colT = sbt("b1colT", [128, 512])
        b2nat = sbt("b2nat", [16, 1024])
        rbcol = sbt("rbcol", [16, 1])
        kssum = sbt("kssum", [16, 1])
        brtcol = sbt("brtcol", [16, 1])
        brtrow = sbt("brtrow", [1, 16])
        wroute_b = sbt("wroute_b", [16, 1024], BF16)
        wrouteT = sbt("wrouteT", [128, 128], BF16)
        Lsb = sbt("Lsb", [128, 16])
        Lwk = sbt("Lwk", [128, 16])
        Mtop = sbt("Mtop", [128, 4])
        masks = sbt("masks", [128, 64])
        Etop = sbt("Etop", [128, 4])
        dsum = sbt("dsum", [128, 1])
        winv = sbt("winv", [128, 1])
        Ptop = sbt("Ptop", [128, 4])
        negm1 = sbt("negm1", [128, 1])
        wgt_tok = sbt("wgt_tok", [128, 16])
        wgt16 = sbt("wgt16", [16, 128])
        wgtrow = sbt("wgtrow", [1, 1024])
        wgt16b = sbt("wgt16b", [16, 512])
        sqrow = sbt("sqrow", [1, 1024])
        keysb = wgtbc[0:16, 0:1024]
        rwb = wgtbc[0:16, 1024:2048]
        wroute = xsq[0:16, 0:1024]
        mrow = wgtrow[0:1, 0:512]
        srow = wgtrow[0:1, 512:1024]
        arow = sqrow[0:1, 0:512]
        brow = sqrow[0:1, 512:1024]
        denrow = sbt("denrow", [1, 128])
        rcprow = sbt("rcprow", [1, 128])
        rbsb = sbt("rbsb", [128, 128])
        Gsb = sbt("Gsb", [128, 65], BF16)

        PS = [psb(f"ps{i}", [128, 512]) for i in range(7)]
        PT = psb("psT", [128, 1024], BF16)

        CONSTS = {-1.0: 0, 0.5: 1, -1.0e30: 2, float(EPS_ATT): 3, float(EPS_LN): 4,
                  -float(LN_SQRT_M): 5, -0.5: 6, float(INV_DK4): 7}
        for val, i in CONSTS.items():
            nc.const_aps.register(F32, float(val), consts[:, i:i + 1])

        ops = {e: [] for e in "tsvgy"}
        sems = {}

        def sget(name):
            if name not in sems:
                sems[name] = [ctx.enter_context(nc.semaphore(name)), 0]
            return sems[name]

        def snow(name):
            return sget(name)[1]

        wmark = {e: {} for e in "tsvgy"}

        def em(eng, fn, inc=None, inc_by=1, waits=()):
            tgt = None
            inch = None
            if inc is not None:
                rec = sget(inc)
                rec[1] += inc_by
                tgt = rec[1]
                inch = rec[0]
            wm = wmark[eng]
            wl = []
            for (s, v) in waits:
                if not v or v <= 0:
                    continue
                if v <= wm.get(s, -1):
                    continue
                wm[s] = v
                wl.append((sget(s)[0], v))

            def run(h, wl=wl, fn=fn, inch=inch, inc_by=inc_by):
                for sh_, v in wl:
                    h.wait_ge(sh_, v)
                ins = fn(h)
                if inch is not None:
                    ins.then_inc(inch, inc_by)
            ops[eng].append(run)
            return tgt

        pbr = [0] * 8

        def bw(i):
            return (f"pb{i}", pbr[i])

        def br(i):
            pbr[i] += 1
            return f"pb{i}"

        def bn(i):
            return snow(f"pb{i}")

        def MM(lhsT, rhs, out_bank, out_ap, start, stop, inc=None, waits=()):
            w = list(waits)
            if start:
                w.append(bw(out_bank))
            return em("t", lambda t, l=lhsT, r=rhs, o=out_ap, st_=start, sp=stop:
                      t.matmul(o, l, r, start=st_, stop=sp, skip_group_check=True),
                      inc=inc, waits=w)

        def dump(name, ap, shape, dt=F32, waits=()):
            if name not in DEBUG:
                return
            d = nc.dram_tensor("dbg_" + name, list(shape), dt,
                               kind="ExternalOutput").ap()
            dbg_outs[name] = "dbg_" + name
            em("g", lambda g, d=d, ap=ap: g.dma_start(d, ap), inc="dbg",
               inc_by=16, waits=waits)

        ffn_st = dict(w1_idx=0, slabs=[], slots=[], wrow={0: 0, 1: 0})

        def setup():
            for val, i in CONSTS.items():
                em("v", lambda v, val=val, i=i: v.memset(consts[:, i:i + 1], float(val)))
            em("v", lambda v: v.memset(ones_r[:], 1.0))
            em("v", lambda v: v.memset(negones_r[:], -1.0))
            em("v", lambda v: v.memset(ones_1[:], 1.0))
            em("v", lambda v: v.memset(meanw[:], 1.0 / D))
            t_ms = em("v", lambda v: v.memset(halfsq_b[:], 0.0625), inc="pv")

            iorow = xsq[:, 0:128]
            iocol = xsq[:, 128:256]
            t_io = em("g", lambda g: g.iota(iorow, [[0, 128]], channel_multiplier=1,
                                            allow_small_or_imprecise_dtypes=True), inc="io")
            t_io = em("g", lambda g: g.iota(iocol, [[1, 128]], channel_multiplier=0,
                                            allow_small_or_imprecise_dtypes=True), inc="io")
            em("v", lambda v: v.tensor_tensor(iden_f[:], iorow, iocol, ALU.is_equal),
               inc="pv", waits=[("io", t_io)])
            em("v", lambda v: v.tensor_tensor(iden_b[:], iorow, iocol, ALU.is_equal),
               inc="pv")
            t_id = em("v", lambda v: v.tensor_tensor(triu_b[:], iorow, iocol, ALU.is_le),
                      inc="pv")


            em("g", lambda g: g.dma_start(keysb[:], keys_d), inc="ld", inc_by=16)
            em("g", lambda g: g.dma_start(rwb[:], rw_d), inc="ld", inc_by=16)
            em("g", lambda g: g.dma_start(
                rbcol[:], rb_d.rearrange("(e a) -> e a", a=1)), inc="ld", inc_by=16)
            em("g", lambda g: g.dma_start(b2nat[:], b2_d), inc="ld", inc_by=16)
            em("g", lambda g: g.dma_start(stage[:, 0:64], proj_d), inc="ld", inc_by=16)
            t_ld = em("g", lambda g: g.dma_start(
                bvrow[:], bv_d.rearrange("(a d) -> a d", a=1)), inc="ld", inc_by=16)
            t_bv = t_ld

            t1 = em("v", lambda v: v.tensor_scalar_mul(stage[:, 64:128], stage[:, 0:64],
                                                       float(INV_DK4)),
                    inc="pv", waits=[("ld", t_ld)])
            t1 = em("v", lambda v: v.tensor_copy(poolB[:, 0:64], stage[:, 64:128]),
                    inc="pv", waits=[("pv", t1)])
            t2 = em("t", lambda t: t.transpose(PT[0:64, 0:128], poolB[:, 0:64],
                                               iden_b[:]),
                    inc="pt", waits=[("pv", t1), bw(7)])
            em("v", lambda v: v.tensor_copy(projT[0:64, :], PT[0:64, 0:128]),
               inc=br(7), waits=[("pt", t2)])
            em("v", lambda v: v.tensor_copy(projT[64:128, :], PT[0:64, 0:128]),
               inc=br(7), waits=[("pt", t2)])
            t_projT = bn(7)

            t4 = em("v", lambda v: v.scalar_tensor_tensor(wroute[:], keysb[:], 2.0,
                                                          rwb[:], ALU.mult, ALU.add),
                    inc="pv", waits=[("ld", t_ld), ("pv", t_id)])
            t4b = em("v", lambda v: v.tensor_copy(wroute_b[:], wroute[:]),
                     inc="pv", waits=[("pv", t4)])
            t5 = em("s", lambda s: s.activation(stage[0:16, 0:1024], keysb[:],
                                                AF.Square),
                    inc="psc", waits=[("ld", t_ld), ("pv", t1)])
            t5 = em("v", lambda v: v.reduce_sum(kssum[:], stage[0:16, 0:1024], axis=AX),
                    inc="pv", waits=[("psc", t5)])
            t5 = em("v", lambda v: v.tensor_sub(brtcol[:], rbcol[:], kssum[:]),
                    inc="pv", waits=[("pv", t5)])
            t6 = em("t", lambda t: t.matmul(PS[0][0:1, 0:16], brtcol[:],
                                            iden_f[0:16, 0:16], start=True, stop=True,
                                            skip_group_check=True),
                    inc="pt", waits=[("pv", t5), ("pv", t_id), bw(0)])
            em("v", lambda v: v.tensor_copy(brtrow[:], PS[0][0:1, 0:16]),
               inc=br(0), waits=[("pt", t6)])
            t_broute = bn(0)

            t_wrT = 0
            for dt in range(NDT):
                t7 = em("t", lambda t, dt=dt: t.transpose(
                    PT[0:128, 0:16], wroute_b[:, dt * 128:(dt + 1) * 128],
                    iden_b[0:16, 0:16]),
                    inc="pt", waits=[("pv", t4b), bw(7)])
                em("v", lambda v, dt=dt: v.tensor_copy(
                    wrouteT[:, dt * 16:(dt + 1) * 16], PT[0:128, 0:16]),
                   inc=br(7), waits=[("pt", t7)])
            t_wrT = bn(7)

            for bsrc, bcol in ((bq_d, bqcol), (bk_d, bkcol), (bo_d, bocol)):
                t8 = em("g", lambda g, b=bsrc: g.dma_start(
                    stage[0:1, 0:1024], b.rearrange("(a d) -> a d", a=1)),
                    inc="ld", inc_by=16,
                    waits=[("pv", snow("pv")), ("pt", snow("pt"))])
                for dt in range(NDT):
                    t9 = em("t", lambda t, dt=dt: t.matmul(
                        PS[1][0:128, 0:1], stage[0:1, dt * 128:(dt + 1) * 128],
                        ones_1[0:1, 0:1], start=True, stop=True,
                        skip_group_check=True),
                        inc="pt", waits=[("ld", t8), bw(1)])
                    em("v", lambda v, dt=dt, c=bcol: v.tensor_copy(
                        c[:, dt:dt + 1], PS[1][0:128, 0:1]),
                       inc=br(1), waits=[("pt", t9)])
            for piece in range(4):
                t10 = em("g", lambda g, p=piece: g.dma_start(
                    stage[0:16, 0:1024], b1_d[:, p * 1024:(p + 1) * 1024]),
                    inc="b1ld", inc_by=16,
                    waits=[("pt", snow("pt")), ("pv", snow("pv"))])
                for j in range(8):
                    ft = piece * 8 + j
                    t11 = em("t", lambda t, j=j: t.matmul(
                        PS[1][0:128, 0:16], stage[0:16, j * 128:(j + 1) * 128],
                        iden_f[0:16, 0:16], start=True, stop=True,
                        skip_group_check=True),
                        inc="pt", waits=[("b1ld", t10), bw(1)])
                    em("v", lambda v, ft=ft: v.tensor_copy(
                        b1colT[:, ft * 16:(ft + 1) * 16], PS[1][0:128, 0:16]),
                       inc=br(1), waits=[("pt", t11)])
            t_bias = bn(1)
            return dict(projT=t_projT, broute=t_broute, wrouteT=t_wrT, bias=t_bias,
                        bvrow=t_bv, ld=snow("ld"),
                        stage_users=[("pt", snow("pt")), ("psc", snow("psc"))],
                        cast_qkv=snow("cast"))

        def emit_casts():
            for e in range(E):
                em("g", lambda g, e=e: g.dma_start(w1b[e], w1_d[e]),
                   inc=f"cast{e}", inc_by=16)
                em("g", lambda g, e=e: g.dma_start(w2b[e], w2_d[e]),
                   inc=f"cast{e}", inc_by=16)
            for dst, srcd in ((wqb, wq_d), (wkb, wk_d), (wvb, wv_d), (wob, wo_d)):
                em("g", lambda g, d=dst, s=srcd: g.dma_start(d, s),
                   inc="castA", inc_by=16)
            stag = attn_dqT
            slot_tok = {0: ("castA", 0), 1: ("castA", 0)}
            ti = 0
            for e in range(E):
                jobs = []
                for fq in range(4):
                    for dt in range(NDT):
                        jobs.append(("w1", dt, fq))
                    for ftl in range(8):
                        jobs.append(("w2", fq * 8 + ftl))
                for job in jobs:
                    p = ti % 2
                    ti += 1
                    if job[0] == "w1":
                        _, dt, fb = job
                        xb = em("s", lambda s, e=e, dt=dt, fb=fb, p=p: s.dma_start(
                            stag[:, p * 1024:(p + 1) * 1024],
                            w1b[e, fb * 1024:(fb + 1) * 1024,
                                dt * 128:(dt + 1) * 128], transpose=True),
                            inc=f"txb{p}", inc_by=16,
                            waits=[(f"cast{e}", 32), slot_tok[p]])
                        st_ = em("g", lambda g, e=e, dt=dt, fb=fb, p=p: g.dma_start(
                            w1t[e, dt * 128:(dt + 1) * 128,
                                fb * 1024:(fb + 1) * 1024],
                            stag[:, p * 1024:(p + 1) * 1024]),
                            inc=f"tw{e}", inc_by=16,
                            waits=[(f"txb{p}", xb)])
                    else:
                        _, ft = job
                        xb = em("s", lambda s, e=e, ft=ft, p=p: s.dma_start(
                            stag[:, p * 1024:(p + 1) * 1024],
                            w2b[e, 0:1024, ft * 128:(ft + 1) * 128],
                            transpose=True),
                            inc=f"txb{p}", inc_by=16,
                            waits=[(f"cast{e}", 32), slot_tok[p]])
                        st_ = em("g", lambda g, e=e, ft=ft, p=p: g.dma_start(
                            w2t[e, ft * 128:(ft + 1) * 128, 0:1024],
                            stag[:, p * 1024:(p + 1) * 1024]),
                            inc=f"tw{e}", inc_by=16,
                            waits=[(f"txb{p}", xb)])
                    slot_tok[p] = (f"tw{e}", st_)
            return ti

        def load_x(b, stage_free, xT_free):
            tch_last = 0
            for tch in range(T // 128):
                w = list(stage_free) + [("xtp", tch_last), ("std", snow("std"))]
                for hlf in range(2):
                    tin = em("g", lambda g, b=b, tch=tch, h=hlf: g.dma_start(
                        stage[:, h * 512:(h + 1) * 512],
                        xin[b, tch * 128:(tch + 1) * 128, h * 512:(h + 1) * 512]),
                        inc="xin", inc_by=16, waits=(w if hlf == 0 else []))
                for dt in range(NDT):
                    ttp = em("t", lambda t, dt=dt: t.transpose(
                        PS[6][0:128, 0:128],
                        stage[:, dt * 128:(dt + 1) * 128],
                        iden_f[:]),
                        inc="xtp",
                        waits=[("xin", tin), bw(6)] + list(xT_free))
                    tch_last = ttp
                    em("v", lambda v, dt=dt, tch=tch: v.tensor_copy(
                        xT[:, dt * T + tch * 128: dt * T + (tch + 1) * 128],
                        PS[6][0:128, 0:128]),
                        inc=br(6), waits=[("xtp", ttp)])
            return ("pb6", bn(6))

        def layer_norm(ci, x_toks, yln_free):
            sv = "lnv"
            ss = "lns_"
            st = "lnt_"
            done = 0
            t4_prev = 0
            t4b_prev = 0
            t_exp_prev = 0
            t3_prev = 0
            for ch in range(NTC):
                cs = ch * 512
                mmq_t = []
                t_mean = None
                for dt in range(NDT):
                    w = list(x_toks) + ([("pb3x", mmq_t[dt - 2])] if dt >= 2 else [])
                    tq = em("s", lambda s, dt=dt, cs=cs: s.activation(
                        xsq[:, (dt % 2) * 512:(dt % 2) * 512 + 512],
                        xT[:, dt * T + cs: dt * T + cs + 512], AF.Square),
                        inc=ss, waits=w)
                    t_mean = MM(meanw[:], xT[:, dt * T + cs: dt * T + cs + 512], 2,
                                PS[2][0:1, 0:512], start=(dt == 0),
                                stop=(dt == NDT - 1),
                                inc=(st if dt == NDT - 1 else None),
                                waits=(list(x_toks) if dt == 0 else []))
                    tmm = MM(meanw[:], xsq[:, (dt % 2) * 512:(dt % 2) * 512 + 512], 3,
                             PS[3][0:1, 0:512], start=(dt == 0), stop=(dt == NDT - 1),
                             inc="pb3x", waits=[(ss, tq)])
                    mmq_t.append(tmm)
                t1 = em("v", lambda v: v.tensor_copy(mrow[:], PS[2][0:1, 0:512]),
                        inc=br(2), waits=[(st, t_mean)])
                t1b = em("v", lambda v: v.tensor_copy(srow[:], PS[3][0:1, 0:512]),
                        inc=br(3), waits=[("pb3x", mmq_t[-1]), ("pb2", t1),
                                          (ss, t_exp_prev)])
                t1 = em("v", lambda v: v.tensor_tensor(arow[:], mrow[:], mrow[:],
                                                       ALU.mult),
                        inc=sv, waits=[("pb3", t1b)])
                t1 = em("v", lambda v: v.tensor_sub(srow[:], srow[:], arow[:]),
                        inc=sv, waits=[(sv, t1)])
                t2 = em("s", lambda s: s.activation(srow[:], srow[:], AF.Ln,
                                                    bias=float(EPS_LN)),
                        inc=ss, waits=[(sv, t1)])
                t2 = em("s", lambda s: s.activation(arow[:], srow[:], AF.Exp,
                                                    scale=-0.5),
                        inc=ss, waits=[(ss, t2), (st, t4_prev), (sv, t3_prev)])
                t_exp_prev = t2
                t3 = em("v", lambda v: v.scalar_tensor_tensor(
                    brow[:], mrow[:], -1.0, arow[:], ALU.mult, ALU.mult),
                    inc=sv, waits=[(ss, t2), (st, t4b_prev)])
                t3_prev = t3
                t4 = MM(ones_r[:], arow[:], 4, PS[4][:, 0:512], True, True,
                        inc=st, waits=[(ss, t2)])
                t4_prev = t4
                t4b = MM(ones_r[:], brow[:], 5, PS[5][:, 0:512], True, True,
                         inc=st, waits=[(sv, t3)])
                t4b_prev = t4b
                for dt in range(NDT):
                    w5 = [(st, t4b), (st, t4)]
                    if dt < 2:
                        w5 += list(yln_free)
                    t5 = em("v", lambda v, dt=dt, cs=cs: v.tensor_tensor(
                        wgtbc[:, (dt % 2) * 512:(dt % 2) * 512 + 512],
                        xT[:, dt * T + cs: dt * T + cs + 512], PS[4][:, 0:512],
                        ALU.mult),
                        inc=br(4), waits=w5)
                    done = em("v", lambda v, dt=dt, cs=cs: v.tensor_tensor(
                        ylnT[:, dt * T + cs: dt * T + cs + 512],
                        wgtbc[:, (dt % 2) * 512:(dt % 2) * 512 + 512], PS[5][:, 0:512],
                        ALU.add),
                        inc=br(5), waits=[("pb4", t5)])
            return ("pb5", bn(5))

        def routing(ci, yln, toks):
            sv = "rtv"
            st = "rtt_"
            t_dma = 0
            t_tp_prev = 0
            t_exp_prev = 0
            for tch in range(NCH):
                for dt in range(NDT):
                    MM(ylnT[:, dt * T + tch * 128: dt * T + (tch + 1) * 128],
                       wrouteT[:, dt * 16:(dt + 1) * 16], 4, PS[4][0:128, 0:16],
                       start=(dt == 0), stop=False,
                       waits=([yln, ("pb7", toks["wrouteT"])] if dt == 0 else []))
                t_lg = MM(ones_r[:], brtrow[:], 4, PS[4][0:128, 0:16],
                          start=False, stop=True, inc=st,
                          waits=[("pb0", toks["broute"])])
                t1 = em("v", lambda v: v.tensor_copy(Lsb[:], PS[4][0:128, 0:16]),
                        inc=br(4), waits=[(st, t_lg)])
                t1 = em("v", lambda v: v.tensor_copy(Lwk[:], Lsb[:]),
                        inc=sv, waits=[("pb4", t1), ("rts_", t_exp_prev),
                                       (st, t_tp_prev)])
                for i in range(4):
                    t1 = em("v", lambda v, i=i: v.reduce_max(
                        Mtop[:, i:i + 1], Lwk[:], axis=AX),
                        inc=sv, waits=[(sv, t1)])
                    t1 = em("v", lambda v, i=i: v.tensor_scalar(
                        masks[:, 16 * i:16 * (i + 1)], Lsb[:], Mtop[:, i:i + 1],
                        None, ALU.is_equal),
                        inc=sv, waits=[(sv, t1)])
                    if i < 3:
                        t1 = em("v", lambda v, i=i: v.scalar_tensor_tensor(
                            Lwk[:], masks[:, 16 * i:16 * (i + 1)], -1.0e30, Lwk[:],
                            ALU.mult, ALU.add),
                            inc=sv, waits=[(sv, t1)])
                t1 = em("v", lambda v: v.tensor_scalar_mul(negm1[:], Mtop[:, 0:1], -1.0),
                        inc=sv, waits=[(sv, t1)])
                t2 = em("s", lambda s: s.activation(Etop[:], Mtop[:], AF.Exp,
                                                    bias=negm1[:]),
                        inc="rts_", waits=[(sv, t1)])
                t_exp_prev = t2
                t3 = em("v", lambda v: v.reduce_sum(dsum[:], Etop[:], axis=AX),
                        inc=sv, waits=[("rts_", t2)])
                t3 = em("v", lambda v: v.reciprocal(winv[:], dsum[:]),
                        inc=sv, waits=[(sv, t3)])
                t3 = em("v", lambda v: v.tensor_scalar(Ptop[:], Etop[:], winv[:], 0.5,
                                                       ALU.mult, ALU.mult),
                        inc=sv, waits=[(sv, t3)])
                t3 = em("v", lambda v: v.tensor_scalar_mul(
                    wgt_tok[:], masks[:, 0:16], Ptop[:, 0:1]),
                    inc=sv, waits=[(sv, t3)])
                for i in range(1, 4):
                    t3 = em("v", lambda v, i=i: v.scalar_tensor_tensor(
                        wgt_tok[:], masks[:, 16 * i:16 * (i + 1)], Ptop[:, i:i + 1],
                        wgt_tok[:], ALU.mult, ALU.add),
                        inc=sv, waits=[(sv, t3)])
                t4 = em("t", lambda t: t.transpose(PS[6][0:16, 0:128], wgt_tok[:],
                                                   iden_f[:]),
                        inc=st, waits=[(sv, t3), bw(6), ("std", snow("std"))])
                t_tp_prev = t4
                t5 = em("v", lambda v: v.tensor_copy(wgt16[:], PS[6][0:16, 0:128]),
                        inc=br(6), waits=[(st, t4), ("rtd_", t_dma)])
                t_dma = em("g", lambda g, ci=ci, tch=tch: g.dma_start(
                    wgtd[ci][:, tch * 128:(tch + 1) * 128], wgt16[:]),
                    inc="rtd_", inc_by=16, waits=[("pb6", t5)])
            return ("rtd_", t_dma)

        def ffn(ci, yln, toks, pool_free, wstream_free):
            rt = routing(ci, yln, toks)
            dump(f"yln{ci}", ylnT[:, 0:NDT * T], [128, NDT * T], BF16, waits=[yln])
            dump(f"wgt{ci}", wgtd[ci], [E, T], F32, waits=[rt])
            sh = "fh_"
            
            so = "fo_"
            sw1 = "fw1"
            sw2 = "fw2"
            sev2 = "fe2_"

            sbb = "fbb_"
            for ch in range(NTC):
                t0 = em("g", lambda g, ci=ci, ch=ch: g.dma_start(
                    wgt16b[:], wgtd[ci][:, ch * 512:(ch + 1) * 512]),
                    inc="fbl_", inc_by=16, waits=[rt, (sbb, snow(sbb))])
                for dot in range(NDT):
                    t1 = MM(b2nat[:, dot * 128:(dot + 1) * 128], wgt16b[:], 6,
                            PS[6][0:128, 0:512], True, True, inc=sbb,
                            waits=[("fbl_", t0), ("ld", toks["ld"]), yln])
                    em("v", lambda v, dot=dot, ch=ch: v.tensor_add(
                        xT[:, dot * T + ch * 512: dot * T + ch * 512 + 512],
                        xT[:, dot * T + ch * 512: dot * T + ch * 512 + 512],
                        PS[6][0:128, 0:512]),
                       inc=br(6), waits=[(sbb, t1)])
            bias_done = ("pb6", bn(6))
            dump(f"xb{ci}", xT[:, 0:NDT * T], [128, NDT * T], F32,
                 waits=[bias_done])

            w1_idx0 = ffn_st["w1_idx"]
            slot0 = len(ffn_st["slots"])
            mm1_of_slab = ffn_st["slabs"]
            mm2_of_slot = ffn_st["slots"]
            wrow_slot_mm = ffn_st["wrow"]
            for e in range(cfg.get("NE", E)):
                
                for ch in range(NTC):
                    fwrp = f"fwr_{ch % 2}"
                    t0 = em("g", lambda g, ci=ci, e=e, ch=ch: g.dma_start(
                        wgtrow[0:1, (ch % 2) * 512:(ch % 2) * 512 + 512],
                        wgtd[ci][e:e + 1, ch * 512:(ch + 1) * 512]),
                        inc=fwrp, inc_by=16,
                        waits=[rt, ("fwb_", wrow_slot_mm[ch % 2])])
                    t1 = MM(ones_r[:], wgtrow[0:1, (ch % 2) * 512:(ch % 2) * 512 + 512],
                            6, PS[6][0:128, 0:512], True, True, inc="fwb_",
                            waits=[(fwrp, t0)])
                    wrow_slot_mm[ch % 2] = t1
                    em("v", lambda v, ch=ch: v.tensor_copy(
                        wgtbc[:, ch * 512:(ch + 1) * 512], PS[6][0:128, 0:512]),
                       inc=br(6), waits=[("fwb_", t1), (sev2, snow(sev2))])
                wgt_ready = bn(6)
                if e < 2:
                    dump(f"wbc{ci}_{e}", wgtbc[:, 0:NTC * 512], [128, NTC * 512],
                         F32, waits=[("pb6", wgt_ready)])

                for fq in range(4):
                    slot = slot0 + e * 4 + fq
                    p2 = slot % 2
                    sw2p = f"{sw2}_{p2}"
                    w2w = [(f"tw{e}", 1024)]
                    if slot >= 2:
                        w2w.append((so, mm2_of_slot[slot - 2]))
                    if e == 0 and fq == 0:
                        w2w += list(pool_free) + list(wstream_free)
                    for ftl in range(8):
                        em("y", lambda y, e=e, fq=fq, ftl=ftl, p2=p2: y.dma_start(
                            poolA[:, p2 * 8192 + ftl * 1024:
                                  p2 * 8192 + (ftl + 1) * 1024],
                            w2t[e, (fq * 8 + ftl) * 128:(fq * 8 + ftl + 1) * 128,
                                0:1024]),
                           inc=sw2p, inc_by=16, waits=(w2w if ftl == 0 else []))
                    w2_tok = (sw2p, snow(sw2p))

                    for ch in range(NTC):
                        cs = ch * 512
                        gel_banks = []
                        for half in range(2):
                            w1_idx = ffn_st["w1_idx"]
                            p1 = w1_idx % 2
                            sw1p = f"{sw1}_{p1}"
                            w1w = [(f"tw{e}", 1024)]
                            if w1_idx >= 2:
                                w1w.append((sh, mm1_of_slab[w1_idx - 2]))
                            if w1_idx == w1_idx0:
                                w1w += list(pool_free) + list(wstream_free)
                            for dt in range(NDT):
                                em("y", lambda y, e=e, fq=fq, half=half, dt=dt, p1=p1:
                                   y.dma_start(
                                    poolC[:, p1 * 4096 + dt * 512:
                                          p1 * 4096 + (dt + 1) * 512],
                                    w1t[e, dt * 128:(dt + 1) * 128,
                                        (fq * 8 + half * 4) * 128:
                                        (fq * 8 + half * 4 + 4) * 128]),
                                   inc=sw1p, inc_by=16,
                                   waits=(w1w if dt == 0 else []))
                            w1_tok = (sw1p, snow(sw1p))
                            for ftl2 in range(4):
                                ft = half * 4 + ftl2
                                ftg = fq * 8 + ft
                                hp = ft % 2
                                for dt in range(NDT):
                                    MM(poolC[:, p1 * 4096 + dt * 512 + ftl2 * 128:
                                             p1 * 4096 + dt * 512 + (ftl2 + 1) * 128],
                                       ylnT[:, dt * T + cs: dt * T + cs + 512],
                                       hp, PS[hp][0:128, 0:512],
                                       start=(dt == 0), stop=(dt == NDT - 1),
                                       inc=(sh if dt == NDT - 1 else None),
                                       waits=([w1_tok, yln] if dt == 0 else []))
                                t_h = snow(sh)
                                gel_banks.append(
                                    em("s", lambda s, ft=ft, hp=hp, e=e, ftg=ftg:
                                       s.activation(
                                        poolB[:, ft * 512:(ft + 1) * 512],
                                        PS[hp][0:128, 0:512], AF.Gelu,
                                        bias=b1colT[:, ftg * 16 + e:
                                                    ftg * 16 + e + 1]),
                                       inc=br(hp),
                                       waits=[(sh, t_h), (so, snow(so)),
                                              ("pb1", toks["bias"])]))
                            mm1_of_slab.append(snow(sh))
                            ffn_st["w1_idx"] += 1
                        g0 = bn(0)
                        g1 = bn(1)
                        for dot in range(NDT):
                            op = 2 + (dot % 2)
                            for ftl3 in range(8):
                                MM(poolA[:, p2 * 8192 + ftl3 * 1024 + dot * 128:
                                         p2 * 8192 + ftl3 * 1024 + (dot + 1) * 128],
                                   poolB[:, ftl3 * 512:(ftl3 + 1) * 512],
                                   op, PS[op][0:128, 0:512],
                                   start=(ftl3 == 0), stop=(ftl3 == 7),
                                   inc=(so if ftl3 == 7 else None),
                                   waits=([("pb0", g0), ("pb1", g1),
                                           w2_tok] if ftl3 == 0 else []))
                            t_o = snow(so)
                            t_p1 = em("v", lambda v, op=op, ch=ch, dot=dot:
                                      v.tensor_tensor(
                                       xsq[:, (dot % 2) * 512:(dot % 2) * 512 + 512],
                                       PS[op][0:128, 0:512],
                                       wgtbc[:, ch * 512:(ch + 1) * 512], ALU.mult),
                                      inc=br(op),
                                      waits=[(so, t_o), ("pb6", wgt_ready),
                                             (sev2, max(0, snow(sev2) - 1))])
                            em("v", lambda v, dot=dot, ch=ch: v.tensor_add(
                                xT[:, dot * T + ch * 512: dot * T + ch * 512 + 512],
                                xT[:, dot * T + ch * 512: dot * T + ch * 512 + 512],
                                xsq[:, (dot % 2) * 512:(dot % 2) * 512 + 512]),
                               inc=sev2,
                               waits=[(f"pb{op}", t_p1), bias_done])
                    mm2_of_slot.append(snow(so))
            return [(sev2, snow(sev2)), ("pb6", bn(6))], (so, snow(so))

        def attn(ab, yln, toks, pool_free, wstream_free):
            sv = "av_"
            st = "at_"
            ss = "as_"
            sd = "ad_"
            sx = "ax_"
            QB, KB, QF, KF, ATM, WTB = 0, T, 2 * T, 3 * T, 4 * T, 4 * T + 128
            KTK, QSQ = 0, 2048
            VOT = 0

            qkv_mm_done = {}
            votok3 = poolC[:, VOT:VOT + NCH * 130].rearrange(
                "p (n c) -> p n c", c=130)
            t_on = em("v", lambda v: v.memset(votok3[:, :, 64:65], 1.0),
                      inc=sv, waits=list(pool_free))
            t_on = em("v", lambda v: v.memset(votok3[:, :, 129:130], 1.0), inc=sv)

            for dqt in range(NDT):
                pw = dqt % 2
                wtb = WTB + pw * 3072
                sdp = f"{sd}_{pw}"
                for mi, mat in enumerate((wqb, wkb, wvb)):
                    for dt in range(NDT):
                        w = [("castA", 64)]
                        if dqt == 0 and mi == 0 and dt == 0:
                            w += list(pool_free) + list(wstream_free)
                        if dqt >= 1 and mi == 0 and dt == 0:
                            w.append((st, qkv_mm_done[dqt - 1]))
                        em("y", lambda y, m=mat, dqt=dqt, dt=dt, mi=mi, wtb=wtb:
                           y.dma_start(
                            poolA[:, wtb + mi * 1024 + dt * 128:
                                  wtb + mi * 1024 + (dt + 1) * 128],
                            m[dqt * 128:(dqt + 1) * 128, dt * 128:(dt + 1) * 128],
                            transpose=True),
                           inc=sdp, inc_by=16, waits=w)
                wv_full = (sdp, snow(sdp))
                wv_toks = {0: wv_full, 1: wv_full, 2: wv_full}
                t_wo = em("y", lambda y, dqt=dqt: y.dma_start(
                    wotb[:], wob[0:1024, dqt * 128:(dqt + 1) * 128], transpose=True),
                    inc=f"{sd}wo", inc_by=16,
                    waits=[("castA", 64), (st, snow(st))])

                qkev_tok = ("pb0", 0)
                for mi, (dst, bcol) in enumerate(((QB, bqcol), (KB, bkcol))):
                    for ch in range(NTC):
                        up = ch % 2
                        for dt in range(NDT):
                            MM(poolA[:, wtb + mi * 1024 + dt * 128:
                                     wtb + mi * 1024 + (dt + 1) * 128],
                               ylnT[:, dt * T + ch * 512: dt * T + ch * 512 + 512],
                               up, PS[up][0:128, 0:512],
                               start=(dt == 0), stop=(dt == NDT - 1),
                               inc=(st if dt == NDT - 1 else None),
                               waits=([wv_toks[mi], yln] if dt == 0 else []))
                        t_qk = snow(st)
                        _sem = br(up)
                        _t = em("s", lambda s, dst=dst, ch=ch, up=up, dqt=dqt,
                                bcol=bcol: s.activation(
                                 poolA[:, dst + ch * 512: dst + ch * 512 + 512],
                                 PS[up][0:128, 0:512], AF.Identity,
                                 bias=bcol[:, dqt:dqt + 1]),
                                inc=_sem, waits=[(st, t_qk)])
                        qkev_tok = (_sem, _t)

                for tch in range(NCH):
                    vp_ = 2 + (tch % 2)
                    for dt in range(NDT):
                        MM(ylnT[:, dt * T + tch * 128: dt * T + (tch + 1) * 128],
                           poolA[:, wtb + 2 * 1024 + dt * 128:
                                 wtb + 2 * 1024 + (dt + 1) * 128],
                           vp_, PS[vp_][0:128, 0:128],
                           start=(dt == 0), stop=False,
                           waits=([wv_toks[2], yln] if dt == 0 else []))
                    t_v = MM(ones_r[:], bvrow[0:1, dqt * 128:(dqt + 1) * 128],
                             vp_, PS[vp_][0:128, 0:128], False, True, inc=st,
                             waits=[("ld", toks["bvrow"])])
                    for hh2 in range(2):
                        em("v", lambda v, tch=tch, hh2=hh2, vp_=vp_: v.tensor_copy(
                            poolC[:, VOT + tch * 130 + hh2 * 65:
                                  VOT + tch * 130 + hh2 * 65 + 64],
                            PS[vp_][0:128, hh2 * 64: hh2 * 64 + 64]),
                           inc=br(vp_), waits=[(st, t_v), (sv, t_on)])
                t_vev = (bn(2), bn(3))

                for hh in range(2):
                    hs = hh * 64
                    for (srcb, dstb) in ((QB, QF), (KB, KF)):
                        t_sq = em("s", lambda s, hs=hs, srcb=srcb: s.activation(
                            poolB[hs:hs + 64, QSQ:QSQ + T],
                            poolA[hs:hs + 64, srcb:srcb + T], AF.Square),
                            inc=ss, waits=[qkev_tok, ("pb7", bn(7)),
                                           (st, snow(st))])
                        for ch in range(NTC):
                            up = ch % 2
                            t_s1 = MM(halfsq_b[hs:hs + 64, :],
                                      poolB[hs:hs + 64,
                                            QSQ + ch * 512: QSQ + ch * 512 + 512],
                                      4, PS[4][0:1, 0:512], True, True, inc=st,
                                      waits=[(ss, t_sq)])
                            t_s2 = em("v", lambda v, ch=ch: v.tensor_copy(
                                sqrow[0:1, (ch % 2) * 512:(ch % 2) * 512 + 512],
                                PS[4][0:1, 0:512]),
                                inc=br(4), waits=[(st, t_s1), (st, snow(st))])
                            MM(projT[hs:hs + 64, :],
                               poolA[hs:hs + 64, srcb + ch * 512:
                                     srcb + ch * 512 + 512],
                               up, PS[up][0:128, 0:512], True, False,
                               waits=[("pb7", toks["projT"]), qkev_tok])
                            t_u = MM(negones_r[:],
                                     sqrow[0:1, (ch % 2) * 512:(ch % 2) * 512 + 512],
                                     up, PS[up][0:128, 0:512], False, True, inc=st,
                                     waits=[("pb4", t_s2)])
                            t_e = em("s", lambda s, dstb=dstb, ch=ch, up=up:
                                     s.activation(
                                      poolA[:, dstb + ch * 512: dstb + ch * 512 + 512],
                                      PS[up][0:128, 0:512], AF.Exp,
                                      bias=-float(LN_SQRT_M)),
                                     inc=br(up), waits=[(st, t_u)])
                            em("v", lambda v, dstb=dstb, ch=ch, up=up:
                               v.tensor_scalar_add(
                                poolA[:, dstb + ch * 512: dstb + ch * 512 + 512],
                                poolA[:, dstb + ch * 512: dstb + ch * 512 + 512],
                                float(EPS_ATT)),
                               inc=sv, waits=[(f"pb{up}", t_e)])
                    t_phi = snow(sv)

                    for n in range(NCH):
                        t_t1 = em("t", lambda t, n=n: t.transpose(
                            PT[0:128, 0:128],
                            poolA[:, KF + n * 128: KF + (n + 1) * 128], iden_b[:]),
                            inc=st, waits=[(sv, t_phi), bw(7)])
                        em("v", lambda v, n=n: v.tensor_copy(
                            poolB[:, KTK + n * 128: KTK + (n + 1) * 128],
                            PT[0:128, 0:128]),
                           inc=br(7), waits=[(st, t_t1)])
                    t_ktk = bn(7)

                    rb_prev = 0
                    for n in range(NCH):
                        vo65 = VOT + n * 130 + hh * 65
                        if n > 0:
                            t_gc = em("v", lambda v: v.tensor_copy(
                                Gsb[:], PS[5][0:128, 0:65]),
                                inc=br(5), waits=[(st, snow(st))])
                        else:
                            t_gc = 0
                        MM(poolB[:, KTK + n * 128: KTK + (n + 1) * 128],
                           poolC[:, vo65:vo65 + 65], 5, PS[5][0:128, 0:65],
                           start=(n == 0), stop=(n == NCH - 1), inc=st,
                           waits=[("pb7", t_ktk), ("pb2", t_vev[0]),
                                  ("pb3", t_vev[1])] +
                                 ([("pb5", t_gc)] if n > 0 else []))
                        t_at = MM(poolA[:, KF + n * 128: KF + (n + 1) * 128],
                                  poolA[:, QF + n * 128: QF + (n + 1) * 128],
                                  4, PS[4][0:128, 0:128], True, True, inc=st,
                                  waits=[(sv, t_phi)])
                        t_atm = em("v", lambda v: v.tensor_tensor(
                            poolA[:, ATM:ATM + 128], PS[4][0:128, 0:128], triu_b[:],
                            ALU.mult),
                            inc=br(4), waits=[(st, t_at)])
                        if n > 0:
                            MM(Gsb[:, 0:64],
                               poolA[:, QF + n * 128: QF + (n + 1) * 128],
                               2, PS[2][hs:hs + 64, 0:128], True, False,
                               waits=[("pb5", t_gc), ("pb4", t_atm)])
                            t_num = MM(poolC[:, vo65:vo65 + 64],
                                       poolA[:, ATM:ATM + 128],
                                       2, PS[2][hs:hs + 64, 0:128], False, True,
                                       inc=st, waits=[])
                            MM(Gsb[:, 64:65],
                               poolA[:, QF + n * 128: QF + (n + 1) * 128],
                               3, PS[3][0:1, 0:128], True, False, waits=[])
                            t_den = MM(poolC[:, vo65 + 64:vo65 + 65],
                                       poolA[:, ATM:ATM + 128],
                                       3, PS[3][0:1, 0:128], False, True,
                                       inc=st, waits=[])
                        else:
                            t_num = MM(poolC[:, vo65:vo65 + 64],
                                       poolA[:, ATM:ATM + 128],
                                       2, PS[2][hs:hs + 64, 0:128], True, True,
                                       inc=st, waits=[("pb4", t_atm)])
                            t_den = MM(poolC[:, vo65 + 64:vo65 + 65],
                                       poolA[:, ATM:ATM + 128],
                                       3, PS[3][0:1, 0:128], True, True,
                                       inc=st, waits=[])
                        t_d1 = em("v", lambda v: v.tensor_scalar_add(
                            denrow[:], PS[3][0:1, 0:128], float(EPS_ATT)),
                            inc=br(3), waits=[(st, t_den)])
                        t_d2 = em("v", lambda v: v.reciprocal(rcprow[:], denrow[:]),
                                  inc=sv, waits=[("pb3", t_d1), (st, rb_prev)])
                        t_rb = MM(ones_r[0:1, 0:64], rcprow[:], 4,
                                  PS[4][hs:hs + 64, 128:256], True, True, inc=st,
                                  waits=[(sv, t_d2)])
                        rb_prev = t_rb
                        t_rbe = em("v", lambda v, hs=hs: v.tensor_copy(
                            rbsb[hs:hs + 64, :], PS[4][hs:hs + 64, 128:256]),
                            inc=br(4), waits=[(st, t_rb)])
                        em("v", lambda v, hs=hs, n=n: v.tensor_tensor(
                            attn_dqT[hs:hs + 64, n * 128:(n + 1) * 128],
                            PS[2][hs:hs + 64, 0:128], rbsb[hs:hs + 64, :],
                            ALU.mult),
                           inc=br(2), waits=[(st, t_num), ("pb4", t_rbe),
                                             toks.get("stag_free", ("castA", 0))])
                t_attn_out = bn(2)

                for dot in range(NDT):
                    for ch in range(NTC):
                        up = ch % 2
                        t_op = MM(wotb[:, dot * 128:(dot + 1) * 128],
                                  attn_dqT[:, ch * 512: ch * 512 + 512],
                                  up, PS[up][0:128, 0:512], True, True, inc=st,
                                  waits=[("pb2", t_attn_out), (f"{sd}wo", t_wo)])
                        if dqt == 0:
                            em("v", lambda v, dot=dot, ch=ch, up=up:
                               v.scalar_tensor_tensor(
                                xT[:, dot * T + ch * 512: dot * T + ch * 512 + 512],
                                PS[up][0:128, 0:512], bocol[:, dot:dot + 1],
                                xT[:, dot * T + ch * 512: dot * T + ch * 512 + 512],
                                ALU.add, ALU.add),
                               inc=br(up), waits=[(st, t_op), ("pb1", toks["bias"])])
                        else:
                            em("v", lambda v, dot=dot, ch=ch, up=up: v.tensor_add(
                                xT[:, dot * T + ch * 512: dot * T + ch * 512 + 512],
                                xT[:, dot * T + ch * 512: dot * T + ch * 512 + 512],
                                PS[up][0:128, 0:512]),
                               inc=br(up), waits=[(st, t_op)])
                qkv_mm_done[dqt] = snow(st)
            return [("pb0", bn(0)), ("pb1", bn(1))], (st, snow(st))

        def store_x(b, x_toks):
            last = 0
            for tch in range(T // 128):
                ev = 0
                for dt in range(NDT):
                    t1 = em("t", lambda t, dt=dt, tch=tch: t.transpose(
                        PS[6][0:128, 0:128],
                        xT[:, dt * T + tch * 128: dt * T + (tch + 1) * 128],
                        iden_f[:]),
                        inc="xtp",
                        waits=list(x_toks) + [bw(6), ("std", snow("std"))])
                    ev = em("v", lambda v, dt=dt: v.tensor_copy(
                        stage[:, dt * 128:(dt + 1) * 128], PS[6][0:128, 0:128]),
                        inc=br(6), waits=[("xtp", t1), ("std", snow("std"))])
                last = em("g", lambda g, b=b, tch=tch: g.dma_start(
                    yout[b, tch * 128:(tch + 1) * 128, :], stage[:]),
                    inc="std", inc_by=16, waits=[("pb6", ev)])
            return ("std", last)

        toks = setup()
        store_tok = None
        casts_emitted = [False]
        prev_tensor_done = []
        prev_scalar_banks = []
        for b in range(NB):
            if b == 0:
                stage_free = list(toks["stage_users"])
                xT_free = []
            else:
                stage_free = [store_tok]
                xT_free = [("xtp", snow("xtp"))]
            x_toks = [load_x(b, stage_free, xT_free)]
            if not casts_emitted[0]:
                emit_casts()
                casts_emitted[0] = True
                toks["stag_free"] = (f"tw{E-1}", snow(f"tw{E-1}"))
            yln_free = prev_tensor_done
            for ph in PHASES:
                if ph == "ffn1":
                    ci = 2 * b
                    yln = layer_norm(f"a{ci}", x_toks, yln_free)
                    x_toks, tdone = ffn(ci, yln, toks, prev_tensor_done,
                                        prev_tensor_done)
                    prev_tensor_done = [tdone]
                    yln_free = [tdone]
                elif ph == "attn":
                    yln = layer_norm(f"b{b}", x_toks, yln_free)
                    dump(f"yln_attn{b}", ylnT[:, 0:NDT * T], [128, NDT * T], BF16,
                         waits=[yln])
                    x_toks, tdone = attn(b, yln, toks, prev_tensor_done,
                                         prev_tensor_done)
                    prev_tensor_done = [tdone]
                    yln_free = [tdone]
                elif ph == "ffn2":
                    ci = 2 * b + 1
                    yln = layer_norm(f"c{ci}", x_toks, yln_free)
                    x_toks, tdone = ffn(ci, yln, toks, prev_tensor_done,
                                        prev_tensor_done)
                    prev_tensor_done = [tdone]
                    yln_free = [tdone]
            store_tok = store_x(b, x_toks)
        em("g", lambda g: g.engine_nop(), waits=[store_tok])

        with nc.Block() as blk:
            @blk.tensor
            def _(h):
                for f_ in ops["t"]:
                    f_(h)

            @blk.scalar
            def _(h):
                for f_ in ops["s"]:
                    f_(h)

            @blk.vector
            def _(h):
                for f_ in ops["v"]:
                    f_(h)

            @blk.gpsimd
            def _(h):
                for f_ in ops["g"]:
                    f_(h)

            @blk.sync
            def _(h):
                for f_ in ops["y"]:
                    f_(h)

    return nc, dbg_outs


def kernel(**inputs):
    cfg = dict(CFG)
    nc, dbg = build_program(cfg)
    NB, T, NC = cfg["NB"], cfg["T"], cfg["NCORES"]
    x = np.ascontiguousarray(np.asarray(inputs["L0_attn_x"], dtype=np.float32))
    in_maps = []
    for c in range(NC):
        m = {k: np.ascontiguousarray(np.asarray(v, dtype=np.float32))
             for k, v in inputs.items() if k != "L0_attn_x"}
        m["L0_attn_x"] = np.ascontiguousarray(x[c * NB:(c + 1) * NB, :T])
        in_maps.append(m)
    res = run_bass_kernel_spmd(nc, in_maps, core_ids=list(range(NC)))
    out = np.concatenate([r["y"] for r in res.results], axis=0)
    return out
